# version 04
# baseline (speedup 1.0000x reference)
import dataclasses
import numpy as np
import concourse.bass as bass
import concourse.mybir as mybir
from concourse.bass_utils import run_bass_kernel_spmd

F32 = mybir.dt.float32
BF16 = mybir.dt.bfloat16
AF = mybir.ActivationFunctionType
ALU = mybir.AluOpType
AX = mybir.AxisListType

NCORE = 8
NT = 1024
HH = 512
M = 32
C = 56
KK = 32
NE = 16
INV_SQRT_D = float(1.0 / np.sqrt(128.0))
THR = 0.001
LN_EPS = 1e-5
PHLIM = 99
AVSTEP = 6
DBG_X = False


class Em:

    QENG = {"wd0": "pool", "wd1": "pool", "wd2": "pool", "wp": "pool",
            "sd0": "sp", "sd1": "sp", "pk0": "sp", "pk1": "sp", "rl": "sp"}

    def __init__(self, nc, mode, sems):
        self.nc = nc
        self.mode = mode
        self.sems = sems
        self.c = {k: 0 for k in
                  ("pe", "dve", "act", "pool", "sp") + tuple(self.QENG)}
        self.last = {}

    def _eng(self, name):
        return {"pe": self.nc.tensor, "dve": self.nc.vector,
                "act": self.nc.scalar, "pool": self.nc.gpsimd,
                "sp": self.nc.sync}[name]

    def op(self, eng, fn):
        if self.mode == eng:
            self.last[eng] = fn(self._eng(eng))

    def mark(self, eng):
        self.c[eng] += 1
        if self.mode == eng:
            self.last[eng].then_inc(self.sems[eng], 1)
        return self.c[eng]

    def dma(self, queue, fn):
        eng = self.QENG[queue]
        self.c[queue] += 16
        if self.mode == eng:
            fn(self._eng(eng)).then_inc(self.sems[queue], 16)
        return self.c[queue]

    def wait(self, eng, sem, val):
        if val <= 0 or eng == sem:
            return
        if self.mode == eng:
            self._eng(eng).wait_ge(self.sems[sem], val)

    def fence(self, eng):
        self.c[eng] += 1
        if self.mode == eng:
            self.last[eng].then_inc(self.sems[eng], 1)
            self._eng(eng).wait_ge(self.sems[eng], self.c[eng])
        return self.c[eng]

    def waitp(self, eng, pair):
        self.wait(eng, pair[0], pair[1])


def repl(ap, dims):
    return dataclasses.replace(ap, ap=[ap.ap[0]] + dims)


def bcast0(ap, reps):
    return dataclasses.replace(ap, ap=list(ap.ap) + [[0, reps]])


def seg32(ap):
    return repl(ap, [[32, 16], [1, 32]])


def build_schedule(em, V):
    S = {}
    regA, regB = V["regA"], V["regB"]
    stage, stagebf, miscbf = V["stage"], V["stagebf"], V["miscbf"]
    sc, gsc = V["sc"], V["gsc"]
    PB, PBT, PBW = V["pb"], V["pbt"], V["pbw"]

    NSLOT = 3
    ring = {"i": 0, "done": [], "cons": []}

    def wslab_push(src_ap):
        i = ring["i"]
        if i >= NSLOT:
            em.wait("pool", "pe", ring["cons"][i - NSLOT])
        slot = i % NSLOT
        w = src_ap.shape[1]
        dst = V["wring"][:, 2048 * slot: 2048 * slot + w]
        v = em.dma("wd%d" % slot,
                   lambda g, d=dst, s=src_ap: g.dma_start(out=d, in_=s))
        ring["done"].append(v)
        ring["cons"].append(None)
        ring["i"] += 1
        return i

    def wslab_wait(i):
        em.wait("pe", "wd%d" % (i % NSLOT), ring["done"][i])

    def wtile(i, t):
        slot = i % NSLOT
        return V["wring"][:, 2048 * slot + 128 * t: 2048 * slot + 128 * t + 128]

    def wslab_done(i):
        ring["cons"][i] = em.mark("pe")

    free = {b: ("dve", 0) for b in range(8)}

    def bank_wait(b, eng="pe"):
        em.waitp(eng, free[b])

    def bank_free(b, sem):
        free[b] = (sem, em.c[sem])

    bfbank = [PBT, PBW]
    bfg = [("dve", 0), ("dve", 0)]
    vp_free = [("pe", 0), ("pe", 0)]
    wtok_free = [("pe", 0)]
    sc_free = [("dve", 0)]

    stg = {("stage", 0): ("dve", 0), ("stage", 1): ("dve", 0),
           ("sbf", 0): ("dve", 0), ("sbf", 1): ("dve", 0),
           ("mbf", 0): ("dve", 0), ("mbf", 1): ("dve", 0)}

    def slot_wait(key, eng):
        em.waitp(eng, stg[key])

    def slot_set(key, sem):
        stg[key] = (sem, em.c[sem])

    def stg_slot(slot):
        return stage[:, 512 * slot:512 * slot + 512]

    def sbf_slot(slot):
        return stagebf[:, 512 * slot:512 * slot + 512]

    def mbf_slot(slot):
        return miscbf[:, 512 * slot:512 * slot + 512]

    def xln(k, hh):
        return regA[:, k * NT + hh * HH: k * NT + hh * HH + HH]

    def hv(c, hh):
        return regB[:, c * NT + hh * HH: c * NT + hh * HH + HH]

    em.dma("rl", lambda s: s.dma_start(out=V["biases"][:], in_=V["d_b"]))
    pre_if = em.dma("rl", lambda s: s.dma_start(out=V["identF"][:],
                                                in_=V["d_id"]))
    em.dma("wp", lambda g: g.dma_start(out=V["identB"][:], in_=V["d_id"]))
    em.dma("wp", lambda g: g.dma_start(out=V["gatewT"][:], in_=V["d_gw"]))
    em.dma("wp", lambda g: g.dma_start(out=V["onesr"][:], in_=V["d_or"]))
    pre_oc = em.dma("wp", lambda g: g.dma_start(out=V["onesc"][:],
                                                in_=V["d_oc"]))
    for eng in ("pe", "dve", "act"):
        em.wait(eng, "wp", pre_oc)
        em.wait(eng, "rl", pre_if)
    em.op("dve", lambda v: v.memset(V["epsc"][:], LN_EPS))
    eps_ready = em.mark("dve")
    em.wait("act", "dve", eps_ready)

    def stats_unit(m, hh, slot, first):
        pass

    def emit_stats_mms(m, hh, slot, first):
        em.op("pe", lambda t: t.matmul(
            PB[2 + hh][0:1, :], V["onesc"][:], sbf_slot(slot),
            start=first, stop=(m == M - 1), skip_group_check=True))
        em.op("pe", lambda t: t.matmul(
            PB[4 + hh][0:1, :], V["onesc"][:], mbf_slot(slot),
            start=first, stop=(m == M - 1), skip_group_check=True))
        em.mark("pe")
        slot_set(("sbf", slot), "pe")
        slot_set(("mbf", slot), "pe")

    def ln_stats_pass(src):
        for b in (2, 3, 4, 5):
            bank_wait(b, "pe")
        first = True
        units = [(m, hh) for m in range(M) for hh in (0, 1)]
        pend = {}

        def issue(u):
            m, hh = units[u]
            slot = u % 2
            slot_wait(("stage", slot), "sp")
            pend[u] = (em.dma("sd%d" % slot, lambda s: s.dma_start(
                out=stg_slot(slot),
                in_=src[:, m * NT + hh * HH: m * NT + hh * HH + HH])), slot)

        issue(0)
        for u, (m, hh) in enumerate(units):
            if True:
                if u + 1 < len(units):
                    issue(u + 1)
                dv, slot = pend.pop(u)
                em.wait("dve", "sd%d" % slot, dv)
                slot_wait(("sbf", slot), "dve")
                em.op("dve", lambda v: v.tensor_copy(sbf_slot(slot),
                                                     stg_slot(slot)))
                cm = em.mark("dve")
                slot_set(("stage", slot), "dve")
                em.wait("act", "dve", cm)
                slot_wait(("mbf", slot), "act")
                em.op("act", lambda a: a.activation(mbf_slot(slot),
                                                    sbf_slot(slot), AF.Square))
                sq = em.mark("act")
                em.wait("pe", "act", sq)
                emit_stats_mms(m, hh, slot, m == 0)
        S["ln_stats_done"] = em.c["pe"]

    def ln_finalize_apply(src, src_ready):
        t_mean = sc[0:1, :]
        t_var = stage[0:1, 0:512]
        t_tmp = stage[0:1, 512:1024]
        for hh in (0, 1):
            em.wait("dve", "pe", S["ln_stats_done"])
            em.waitp("dve", sc_free[0])
            slot_wait(("stage", 0), "dve")
            slot_wait(("stage", 1), "dve")
            em.op("dve", lambda v: v.tensor_scalar(
                t_mean, PB[2 + hh][0:1, :], 1.0 / 4096.0, None, ALU.mult))
            em.fence("dve")
            em.op("dve", lambda v: v.tensor_scalar(
                t_var, PB[4 + hh][0:1, :], 1.0 / 4096.0, None, ALU.mult))
            em.fence("dve")
            em.op("dve", lambda v: v.tensor_mul(t_tmp, t_mean, t_mean))
            em.fence("dve")
            em.op("dve", lambda v: v.tensor_sub(t_var, t_var, t_tmp))
            em.fence("dve")
            em.op("dve", lambda v: v.tensor_copy(V["m16"][:], t_mean))
            f1 = em.mark("dve")
            em.wait("act", "dve", f1)
            em.op("act", lambda a: a.activation(t_tmp, t_var, AF.Sqrt,
                                                bias=V["epsc"][0:1, :]))
            f2 = em.mark("act")
            em.wait("dve", "act", f2)
            em.op("dve", lambda v: v.reciprocal(t_var, t_tmp))
            em.fence("dve")
            em.op("dve", lambda v: v.tensor_copy(V["r16"][:], t_var))
            f3 = em.mark("dve")
            sc_free[0] = ("dve", em.c["dve"])
            slot_set(("stage", 0), "dve")
            slot_set(("stage", 1), "dve")
            bank_free(2 + hh, "dve")
            bank_free(4 + hh, "dve")
            em.wait("pe", "dve", f3)
            bank_wait(0, "pe")
            bank_wait(1, "pe")
            em.op("pe", lambda t: t.matmul(PB[0][:], V["onesr"][:],
                                           V["m16"][:], start=True, stop=True))
            em.op("pe", lambda t: t.matmul(PB[1][:], V["onesr"][:],
                                           V["r16"][:], start=True, stop=True))
            bc = em.mark("pe")
            apend = {}

            def aissue(m, hh=hh):
                slot = m % 2
                slot_wait(("stage", slot), "sp")
                apend[m] = (em.dma("sd%d" % slot, lambda s: s.dma_start(
                    out=stg_slot(slot),
                    in_=src[:, m * NT + hh * HH: m * NT + hh * HH + HH])),
                    slot)

            if src_ready:
                em.wait("sp", "sd0", em.c["sd0"])
                em.wait("sp", "sd1", em.c["sd1"])
                src_ready = False
            aissue(0)
            for m in range(M):
                if m + 1 < M:
                    aissue(m + 1)
                dv, slot = apend.pop(m)
                em.wait("dve", "sd%d" % slot, dv)
                em.wait("dve", "pe", bc)
                em.op("dve", lambda v: v.tensor_sub(
                    stg_slot(slot), stg_slot(slot), PB[0][:]))
                em.op("dve", lambda v: v.tensor_mul(
                    xln(m, hh), stg_slot(slot), PB[1][:]))
                em.mark("dve")
                slot_set(("stage", slot), "dve")
            bank_free(0, "dve")
            bank_free(1, "dve")
        S["xln_ready"] = em.c["dve"]

    def gating():
        em.wait("pe", "dve", S["xln_ready"])
        pass
        bank_wait(2, "pe")
        bank_wait(3, "pe")
        for hh in (0, 1):
            for k in range(KK):
                em.op("pe", lambda t: t.matmul(
                    PB[2 + hh][0:16, :], V["gatewT"][:, 16 * k:16 * k + 16],
                    xln(k, hh), start=(k == 0), stop=(k == KK - 1)))
        lg = em.mark("pe")
        em.wait("dve", "pe", lg)
        slot_wait(("stage", 0), "dve")
        slot_wait(("stage", 1), "dve")
        em.op("dve", lambda v: v.tensor_copy(stage[0:16, 0:512],
                                             PB[2][0:16, :]))
        em.op("dve", lambda v: v.tensor_copy(stage[0:16, 512:1024],
                                             PB[3][0:16, :]))
        lge = em.mark("dve")
        bank_free(2, "dve")
        bank_free(3, "dve")
        em.wait("pe", "dve", lge)
        bank_wait(4, "pe")
        for b in range(8):
            hh, bb = divmod(b, 4)
            em.op("pe", lambda t: t.transpose(
                PB[4][:, 16 * b:16 * b + 16],
                stage[0:16, hh * 512 + 128 * bb: hh * 512 + 128 * bb + 128],
                V["identF"][0:16, 0:16]))
        ltr = em.mark("pe")
        em.wait("dve", "pe", ltr)
        em.op("dve", lambda v: v.tensor_copy(V["ltok"][:], PB[4][:, 0:128]))
        em.mark("dve")
        bank_free(4, "dve")
        slot_set(("stage", 0), "dve")
        slot_set(("stage", 1), "dve")
        em.waitp("dve", wtok_free[0])
        for b in range(8):
            lt = V["ltok"][:, 16 * b:16 * b + 16]
            c_m1, c_m2 = gsc[:, 0:1], gsc[:, 1:2]
            c_e2, c_den, c_r = gsc[:, 2:3], gsc[:, 3:4], gsc[:, 4:5]
            r_t, r_u, r_x = gsc[:, 16:32], gsc[:, 32:48], gsc[:, 48:64]
            em.op("dve", lambda v: v.reduce_max(c_m1, lt, axis=AX.X))
            em.fence("dve")
            em.op("dve", lambda v: v.tensor_scalar(r_t, lt, c_m1, None,
                                                   ALU.subtract))
            em.fence("dve")
            em.op("dve", lambda v: v.tensor_scalar(r_u, r_t, 0.0, None,
                                                   ALU.is_ge))
            em.fence("dve")
            em.op("dve", lambda v: v.scalar_tensor_tensor(
                r_x, r_u, -1e30, r_t, ALU.mult, ALU.add))
            em.fence("dve")
            em.op("dve", lambda v: v.reduce_max(c_m2, r_x, axis=AX.X))
            g1 = em.mark("dve")
            em.wait("act", "dve", g1)
            em.op("act", lambda a: a.activation(c_e2, c_m2, AF.Exp))
            em.fence("act")
            em.op("act", lambda a: a.activation(r_x, r_t, AF.Exp))
            g2 = em.mark("act")
            em.wait("dve", "act", g2)
            em.op("dve", lambda v: v.tensor_scalar(c_den, c_e2, 1.0, None,
                                                   ALU.add))
            em.fence("dve")
            em.op("dve", lambda v: v.reciprocal(c_r, c_den))
            em.fence("dve")
            em.op("dve", lambda v: v.tensor_scalar(r_u, r_t, c_m2, None,
                                                   ALU.is_ge))
            em.fence("dve")
            em.op("dve", lambda v: v.tensor_mul(r_t, r_u, r_x))
            em.fence("dve")
            em.op("dve", lambda v: v.tensor_scalar(
                V["wtok"][:, 16 * b:16 * b + 16], r_t, c_r, 0.5,
                ALU.mult, ALU.mult))
            em.mark("dve")
        S["wtok_ready"] = em.c["dve"]

    fin_at = {}
    for e in range(NE):
        fin_at.setdefault((448 * e + 447) // 128, []).append(e)

    def expert_spans(e):
        lo, hi = 448 * e, 448 * (e + 1)
        spans = []
        c0 = lo // 128
        if lo % 128:
            spans.append((slice(64, 128), c0, 1))
            c0 += 1
        nfull = hi // 128 - c0
        if nfull:
            spans.append((slice(0, 128), c0, nfull))
        if hi % 128:
            spans.append((slice(0, 64), hi // 128, 1))
        return spans

    def hspan_ap(psl, c0, hh, nch):
        base = regB[psl, c0 * NT + hh * HH: c0 * NT + hh * HH + HH]
        return repl(base, [[NT, nch], [1, HH]])

    def wspan_ap(psl, nch, hh2):
        return repl(PB[2 + hh2][psl, :], [[0, nch], [1, HH]])

    def ffn_gu(wg_src, wu_src):
        hmul = {}
        for c in range(C):
            sg0 = wslab_push(wg_src[c, :, 0:2048])
            sg1 = wslab_push(wg_src[c, :, 2048:4096])
            su0 = wslab_push(wu_src[c, :, 0:2048])
            gb0, gb1 = PB[0], PB[1]
            em.wait("pe", "dve", S["xln_ready"])
            bank_wait(0, "pe")
            bank_wait(1, "pe")
            for k in range(KK):
                slab = sg0 if k < 16 else sg1
                if k in (0, 16):
                    wslab_wait(slab)
                em.op("pe", lambda t: t.matmul(
                    gb0[:], wtile(slab, k % 16), xln(k, 0),
                    start=(k == 0), stop=(k == KK - 1)))
                em.op("pe", lambda t: t.matmul(
                    gb1[:], wtile(slab, k % 16), xln(k, 1),
                    start=(k == 0), stop=(k == KK - 1)))
                if k == 15:
                    wslab_done(sg0)
            wslab_done(sg1)
            su1 = wslab_push(wu_src[c, :, 2048:4096])
            gdone = em.c["pe"]
            bank_wait(4, "pe")
            bank_wait(5, "pe")
            for k in range(KK):
                slab = su0 if k < 16 else su1
                if k in (0, 16):
                    wslab_wait(slab)
                em.op("pe", lambda t: t.matmul(
                    PB[4][:], wtile(slab, k % 16), xln(k, 0),
                    start=(k == 0), stop=(k == KK - 1)))
                em.op("pe", lambda t: t.matmul(
                    PB[5][:], wtile(slab, k % 16), xln(k, 1),
                    start=(k == 0), stop=(k == KK - 1)))
                if k == 15:
                    wslab_done(su0)
            wslab_done(su1)
            udone = em.c["pe"]
            for hh in (0, 1):
                gb = gb0 if hh == 0 else gb1
                ub = PB[4 + hh]
                em.wait("act", "pe", gdone)
                slot_wait(("mbf", hh), "act")
                em.op("act", lambda a: a.activation(mbf_slot(hh), gb[:],
                                                    AF.Silu))
                sm = em.mark("act")
                bank_free(hh, "act")
                em.wait("dve", "act", sm)
                em.wait("dve", "pe", udone)
                em.op("dve", lambda v: v.tensor_mul(hv(c, hh), mbf_slot(hh),
                                                    ub[:]))
                hmul[(c, hh)] = em.mark("dve")
                slot_set(("mbf", hh), "dve")
                bank_free(4 + hh, "dve")
            for e in fin_at.get(c, []):
                em.wait("pe", "dve", S["wtok_ready"])
                em.waitp("pe", bfg[0])
                em.waitp("pe", bfg[1])
                for b in range(8):
                    em.op("pe", lambda t: t.transpose(
                        PBT[0:1, 128 * b:128 * b + 128],
                        V["wtok"][:, 16 * b + e:16 * b + e + 1],
                        V["identB"][:]))
                wtr = em.mark("pe")
                em.wait("dve", "pe", wtr)
                em.op("dve", lambda v: v.tensor_copy(V["wrow16"][0:1, 0:512],
                                                     PBT[0:1, 0:512]))
                em.op("dve", lambda v: v.tensor_copy(
                    V["wrow16"][0:1, 512:1024], PBT[0:1, 512:1024]))
                wre = em.mark("dve")
                bfg[0] = ("dve", em.c["dve"])
                em.wait("pe", "dve", wre)
                wbs = []
                for hh2 in (0, 1):
                    bank_wait(2 + hh2, "pe")
                    em.op("pe", lambda t: t.matmul(
                        PB[2 + hh2][:], V["onesr"][:],
                        V["wrow16"][0:1, 512 * hh2:512 * hh2 + 512],
                        start=True, stop=True))
                    wbs.append(em.mark("pe"))
                for hh2 in (0, 1):
                    em.wait("dve", "pe", wbs[hh2])
                    for (psl, c0, nch) in expert_spans(e):
                        em.op("dve", lambda v: v.tensor_mul(
                            hspan_ap(psl, c0, hh2, nch),
                            hspan_ap(psl, c0, hh2, nch),
                            wspan_ap(psl, nch, hh2)))
                    em.mark("dve")
                    bank_free(2 + hh2, "dve")
        wtok_free[0] = ("pe", em.c["pe"])
        S["h_ready"] = em.c["dve"]

    def ffn_down(wd_src, x_src, dst_dram, stats):
        if stats:
            for b in (2, 3, 4, 5):
                bank_wait(b, "pe")
        wv = 0
        dpend = {}

        def dissue(u):
            m2, hh2 = divmod(u, 2)
            slot = u % 2
            slot_wait(("stage", slot), "sp")
            dpend[u] = (em.dma("sd%d" % slot, lambda s: s.dma_start(
                out=stg_slot(slot),
                in_=x_src[:, m2 * NT + hh2 * HH: m2 * NT + hh2 * HH + HH])),
                slot)

        for m in range(M):
            if stats:
                i0, i1 = 0, 1
            else:
                i0, i1 = 2 * (m % 2), 2 * (m % 2) + 1
            b0, b1 = PB[i0], PB[i1]
            sl = [wslab_push(wd_src[m, :, 2048 * q: 2048 * q + 2048])
                  for q in range(3)]
            em.wait("pe", "dve", S["h_ready"])
            bank_wait(i0, "pe")
            bank_wait(i1, "pe")
            for c in range(C):
                q, t = divmod(c, 16)
                if c == 16:
                    sl.append(wslab_push(wd_src[m, :, 6144:7168]))
                if t == 0:
                    wslab_wait(sl[q])
                em.op("pe", lambda te: te.matmul(
                    b0[:], wtile(sl[q], t), hv(c, 0),
                    start=(c == 0), stop=(c == C - 1)))
                em.op("pe", lambda te: te.matmul(
                    b1[:], wtile(sl[q], t), hv(c, 1),
                    start=(c == 0), stop=(c == C - 1)))
                if t == 15 or c == C - 1:
                    wslab_done(sl[q])
            ddone = em.c["pe"]
            for hh in (0, 1):
                bank = b0 if hh == 0 else b1
                u = 2 * m + hh
                if u == 0:
                    dissue(0)
                if u + 1 < 2 * M:
                    dissue(u + 1)
                dv, slot = dpend.pop(u)
                em.wait("dve", "sd%d" % slot, dv)
                em.wait("dve", "pe", ddone)
                em.op("dve", lambda v: v.tensor_add(stg_slot(slot),
                                                    stg_slot(slot), bank[:]))
                if stats:
                    slot_wait(("sbf", slot), "dve")
                    em.op("dve", lambda v: v.tensor_copy(sbf_slot(slot),
                                                         stg_slot(slot)))
                ev = em.mark("dve")
                bank_free(i0 + hh, "dve")
                em.wait("sp", "dve", ev)
                wv = em.dma("sd%d" % slot, lambda s: s.dma_start(
                    out=dst_dram[:, m * NT + hh * HH: m * NT + hh * HH + HH],
                    in_=stg_slot(slot)))
                slot_set(("stage", slot), "sd%d" % slot)
                if stats:
                    em.wait("act", "dve", ev)
                    slot_wait(("mbf", slot), "act")
                    em.op("act", lambda a: a.activation(
                        mbf_slot(slot), sbf_slot(slot), AF.Square))
                    sq = em.mark("act")
                    em.wait("pe", "act", sq)
                    emit_stats_mms(m, hh, slot, m == 0)
        if stats:
            S["ln_stats_done"] = em.c["pe"]
        return wv

    def proj(w_src, kind, rhs_fn, bias_idx, x_src=None, dst_dram=None,
             park=None, stats=False):
        evmarks, parkw = [], []
        wrlast = 0
        opend = {}

        def oissue(u):
            m2, hh2 = divmod(u, 2)
            slot = u % 2
            slot_wait(("stage", slot), "sp")
            opend[u] = (em.dma("sd%d" % slot, lambda s: s.dma_start(
                out=stg_slot(slot),
                in_=x_src[:, m2 * NT + hh2 * HH: m2 * NT + hh2 * HH + HH])),
                slot)

        if stats:
            for b in (2, 3, 4, 5):
                bank_wait(b, "pe")
        for m in range(M):
            s0 = wslab_push(w_src[m, :, 0:2048])
            s1 = wslab_push(w_src[m, :, 2048:4096])
            if stats:
                i0, i1 = 0, 1
            else:
                i0, i1 = 2 * (m % 2), 2 * (m % 2) + 1
            b0, b1 = PB[i0], PB[i1]
            em.waitp("pe", S["rhs_ready"])
            bank_wait(i0, "pe")
            bank_wait(i1, "pe")
            for k in range(KK):
                slab = s0 if k < 16 else s1
                if k in (0, 16):
                    wslab_wait(slab)
                em.op("pe", lambda t: t.matmul(
                    b0[:], wtile(slab, k % 16), rhs_fn(k, 0),
                    start=(k == 0), stop=(k == KK - 1)))
                em.op("pe", lambda t: t.matmul(
                    b1[:], wtile(slab, k % 16), rhs_fn(k, 1),
                    start=(k == 0), stop=(k == KK - 1)))
                if k == 15:
                    wslab_done(s0)
            wslab_done(s1)
            pdone = em.c["pe"]
            bias = V["biases"][:, 32 * bias_idx + m: 32 * bias_idx + m + 1]
            if kind in ("q", "k"):
                base = 0 if kind == "q" else 16384
                em.wait("act", "pe", pdone)
                em.op("act", lambda a: a.activation(
                    regB[:, base + m * HH: base + m * HH + HH], b0[:],
                    AF.Identity, bias=bias))
                e0 = em.mark("act")
                bank_free(i0, "act")
                slot = m % 2
                slot_wait(("mbf", slot), "act")
                em.op("act", lambda a: a.activation(mbf_slot(slot), b1[:],
                                                    AF.Identity, bias=bias))
                e1 = em.mark("act")
                bank_free(i1, "act")
                em.wait("sp", "act", e1)
                pw = em.dma("pk%d" % slot, lambda s: s.dma_start(
                    out=park[:, m * HH: m * HH + HH], in_=mbf_slot(slot)))
                parkw.append(pw)
                slot_set(("mbf", slot), "pk%d" % slot)
                evmarks.append(e1)
            elif kind == "v":
                em.wait("act", "pe", pdone)
                if m == 0:
                    em.wait("act", "pe", S["scores_h1_done"])
                for hh in (0, 1):
                    bank = b0 if hh == 0 else b1
                    base = regB[:, 16384 + hh * 16384 + m:
                                16384 + hh * 16384 + m + 1]
                    dst = repl(base, [[32, HH]])
                    em.op("act", lambda a: a.activation(dst, bank[:], AF.Identity,
                                                        bias=bias))
                    evmarks.append(em.mark("act"))
                    bank_free(i0 + hh, "act")
            elif kind == "o":
                for hh in (0, 1):
                    bank = b0 if hh == 0 else b1
                    slot = (2 * m + hh) % 2
                    em.wait("act", "pe", pdone)
                    em.waitp("act", sc_free[0])
                    em.op("act", lambda a: a.activation(sc[:, :], bank[:],
                                                        AF.Identity, bias=bias))
                    ab = em.mark("act")
                    bank_free(i0 + hh, "act")
                    u = 2 * m + hh
                    if u == 0:
                        oissue(0)
                    if u + 1 < 2 * M:
                        oissue(u + 1)
                    dv, slot = opend.pop(u)
                    em.wait("dve", "sd%d" % slot, dv)
                    em.wait("dve", "act", ab)
                    em.op("dve", lambda v: v.tensor_add(stg_slot(slot),
                                                        stg_slot(slot),
                                                        sc[:, :]))
                    if stats:
                        slot_wait(("sbf", slot), "dve")
                        em.op("dve", lambda v: v.tensor_copy(sbf_slot(slot),
                                                             stg_slot(slot)))
                    ev = em.mark("dve")
                    sc_free[0] = ("dve", em.c["dve"])
                    em.wait("sp", "dve", ev)
                    wrlast = em.dma("sd%d" % slot, lambda s: s.dma_start(
                        out=dst_dram[:, m * NT + hh * HH:
                                     m * NT + hh * HH + HH],
                        in_=stg_slot(slot)))
                    slot_set(("stage", slot), "sd%d" % slot)
                    if stats:
                        em.wait("act", "dve", ev)
                        slot_wait(("mbf", slot), "act")
                        em.op("act", lambda a: a.activation(
                            mbf_slot(slot), sbf_slot(slot), AF.Square))
                        sq = em.mark("act")
                        em.wait("pe", "act", sq)
                        emit_stats_mms(m, hh, slot, m == 0)
        if stats:
            S["ln_stats_done"] = em.c["pe"]
        return evmarks, parkw, wrlast

    def scores_half(half, qk_ready):
        for kb in range(8):
            t0 = 64 * kb
            bank = PB[kb % 2]
            bank_wait(kb % 2, "pe")
            em.waitp("pe", qk_ready)
            for s in range(64):
                a = s % 4
                b = s // 4
                tl = t0 + s
                qs = regB[:, tl:16384:HH]
                ks = regB[:, 16384 + tl:32768:HH]
                em.op("pe", lambda t: t.matmul(
                    bank[32 * a:32 * a + 32, 32 * b:32 * b + 32], qs, ks,
                    start=True, stop=True,
                    tile_position=(0, 32 * a), skip_group_check=True))
            mm = em.mark("pe")
            em.wait("dve", "pe", mm)
            em.waitp("dve", sc_free[0])
            em.op("dve", lambda v: v.tensor_scalar(
                sc[:, :], bank[:], THR / INV_SQRT_D, 100.0, ALU.is_gt,
                ALU.mult))
            em.fence("dve")
            em.op("dve", lambda v: v.scalar_tensor_tensor(
                sc[:, :], bank[:], INV_SQRT_D, sc[:, :], ALU.mult, ALU.add))
            em.fence("dve")
            em.op("dve", lambda v: v.reduce_max(gsc[:, 0:16],
                                                seg32(sc[:, :]), axis=AX.X))
            em.fence("dve")
            em.op("dve", lambda v: v.tensor_sub(
                seg32(sc[:, :]), seg32(sc[:, :]), bcast0(gsc[:, 0:16], 32)))
            d1 = em.mark("dve")
            bank_free(kb % 2, "dve")
            em.wait("act", "dve", d1)
            slot_wait(("sbf", 0), "act")
            em.op("act", lambda a: a.activation(stagebf[:, 0:512], sc[:, :],
                                                AF.Exp))
            ex = em.mark("act")
            sc_free[0] = ("act", em.c["act"])
            em.wait("dve", "act", ex)
            slot_wait(("sbf", 1), "dve")
            em.op("dve", lambda v: v.reduce_sum(
                gsc[:, 16:32], seg32(stagebf[:, 0:512]), axis=AX.X))
            em.fence("dve")
            em.op("dve", lambda v: v.reciprocal(gsc[:, 32:48], gsc[:, 16:32]))
            em.fence("dve")
            em.op("dve", lambda v: v.tensor_mul(
                seg32(stagebf[:, 512:1024]), seg32(stagebf[:, 0:512]),
                bcast0(gsc[:, 32:48], 32)))
            em.fence("dve")
            atcol = 49152 + (8 * half + kb) * 512
            for j in range(16):
                em.op("dve", lambda v: v.transpose(
                    regB[:, atcol + 32 * j: atcol + 32 * j + 32],
                    stagebf[:, 512 + 32 * j: 512 + 32 * j + 32]))
            em.mark("dve")
            slot_set(("sbf", 0), "dve")
            slot_set(("sbf", 1), "dve")
        S["scores_done_%d" % half] = em.c["pe"]
        S["attnT_done_%d" % half] = em.c["dve"]

    def av_phase():
        em.wait("pe", "act", S["v_evac_done"])
        em.wait("pe", "dve", S["attnT_done_1"])
        for grp in range(64):
            t0 = 16 * grp
            avb = 4 + grp % 2
            avmm = 0
            for i in range(4):
                bk = i % 2
                em.waitp("pe", bfg[bk])
                em.op("pe", lambda t: t.transpose(
                    bfbank[bk][:, 0:128],
                    regB[:, 16384 + (t0 + 4 * i) * 32:
                         16384 + (t0 + 4 * i) * 32 + 128],
                    V["identB"][:]))
                vtr = em.mark("pe")
                em.wait("act", "pe", vtr)
                vslot = i % 2
                em.waitp("act", vp_free[vslot])
                em.op("act", lambda a: a.activation(
                    V["vpack"][:, 128 * vslot:128 * vslot + 128],
                    bfbank[bk][:, 0:128], AF.Identity))
                vpe = em.mark("act")
                bfg[bk] = ("act", em.c["act"])
                em.wait("pe", "act", vpe)
                if i == 0:
                    bank_wait(avb, "pe")
                for j in range(4):
                    tg = t0 + 4 * i + j
                    kbg = tg // 64
                    s64 = tg % 64
                    acol = 49152 + kbg * 512 + 32 * (s64 // 4)
                    em.op("pe", lambda t: t.matmul(
                        PB[avb][32 * j:32 * j + 32, 128 * i:128 * i + 128],
                        regB[32 * j:32 * j + 32, acol:acol + 32],
                        V["vpack"][32 * j:32 * j + 32,
                                   128 * vslot:128 * vslot + 128],
                        start=True, stop=True,
                        tile_position=(32 * j, 32 * j),
                        skip_group_check=True))
                avmm = em.mark("pe")
                vp_free[vslot] = ("pe", em.c["pe"])
            em.wait("dve", "pe", avmm)
            slot_wait(("mbf", 0), "dve")
            em.op("dve", lambda v: v.tensor_copy(miscbf[:, 0:512],
                                                 PB[avb][:]))
            ae = em.mark("dve")
            bank_free(avb, "dve")
            for i in range(4):
                bk = i % 2
                em.waitp("pe", bfg[bk])
                em.wait("pe", "dve", ae)
                em.op("pe", lambda t: t.transpose(
                    bfbank[bk][:, 128:256],
                    miscbf[:, 128 * i:128 * i + 128], V["identB"][:]))
                otr = em.mark("pe")
                em.wait("dve", "pe", otr)
                srcap = repl(bfbank[bk][:, 128:129], [[32, 4], [1, 32]])
                dst = repl(regA[:, t0 + 4 * i: t0 + 4 * i + 1],
                           [[1, 4], [NT, 32]])
                em.op("dve", lambda v: v.tensor_copy(dst, srcap))
                em.mark("dve")
                bfg[bk] = ("dve", em.c["dve"])
            slot_set(("mbf", 0), "dve")
        S["ao_ready"] = em.c["dve"]

    def fin():
        if DBG_X:
            em.wait("sp", "pe", em.c["pe"])
            em.wait("sp", "dve", em.c["dve"])
            em.wait("sp", "act", em.c["act"])
            em.dma("rl", lambda s: s.dma_start(out=V["d_dbgA"], in_=regA[:, :]))
            em.dma("rl", lambda s: s.dma_start(out=V["d_dbgB"], in_=regB[:, :]))
            em.dma("rl", lambda s: s.dma_start(out=V["d_dbgW"],
                                               in_=V["wtok"][:, :]))
            em.dma("rl", lambda s: s.dma_start(out=V["d_dbgL"],
                                               in_=V["ltok"][:, :]))
        for q in Em.QENG:
            em.wait("sp", q, em.c[q])

    ln_stats_pass(V["d_x"])
    if PHLIM <= 0:
        return fin()
    ln_finalize_apply(V["d_x"], False)
    if PHLIM <= 1:
        return fin()
    gating()
    if PHLIM <= 2:
        return fin()
    ffn_gu(V["d_g"], V["d_u"])
    if PHLIM <= 3:
        return fin()
    x1w = ffn_down(V["d_d"], V["d_x"], V["d_x1"], stats=True)
    if PHLIM <= 4:
        return fin()
    ln_finalize_apply(V["d_x1"], True)
    S["rhs_ready"] = ("dve", S["xln_ready"])
    if PHLIM <= 5:
        return fin()
    qe, qpw, _ = proj(V["d_wq"], "q", xln, 0, park=V["d_qp"])
    ke, kpw, _ = proj(V["d_wk"], "k", xln, 1, park=V["d_kp"])
    if PHLIM <= 6:
        return fin()
    scores_half(0, ("act", ke[-1]))
    if PHLIM <= 7:
        return fin()
    em.wait("sp", "pk0", em.c["pk0"])
    em.wait("sp", "pk1", em.c["pk1"])
    em.wait("sp", "pe", S["scores_done_0"])
    em.dma("rl", lambda s: s.dma_start(out=regB[:, 0:16384],
                                       in_=V["d_qp"][:, :]))
    rl2 = em.dma("rl", lambda s: s.dma_start(out=regB[:, 16384:32768],
                                             in_=V["d_kp"][:, :]))
    scores_half(1, ("rl", rl2))
    S["scores_h1_done"] = S["scores_done_1"]
    if PHLIM <= 8:
        return fin()
    ve, _, _ = proj(V["d_wv"], "v", xln, 2)
    S["v_evac_done"] = ve[-1]
    if PHLIM <= 9:
        return fin()
    av_phase()
    S["rhs_ready"] = ("dve", S["ao_ready"])
    if PHLIM <= 10:
        return fin()

    def ao_rhs(k, hh):
        return regA[:, k * NT + hh * HH: k * NT + hh * HH + HH]

    _, _, x2w = proj(V["d_wo"], "o", ao_rhs, 3, x_src=V["d_x1"],
                     dst_dram=V["d_x2"], stats=True)
    if PHLIM <= 11:
        return fin()
    ln_finalize_apply(V["d_x2"], True)
    if PHLIM <= 12:
        return fin()
    gating()
    ffn_gu(V["d_g"], V["d_u"])
    if PHLIM <= 13:
        return fin()
    ffn_down(V["d_d"], V["d_x2"], V["d_y"], stats=False)
    fin()


def build_nc():
    nc = bass.Bass("TRN2", target_bir_lowering=False,
                   detect_race_conditions=False)
    V = {}
    V["d_x"] = nc.dram_tensor("xin", [128, M * NT], F32,
                              kind="ExternalInput").ap()
    V["d_y"] = nc.dram_tensor("y", [128, M * NT], F32,
                              kind="ExternalOutput").ap()
    V["d_g"] = nc.dram_tensor("wg", [C, 128, 4096], F32,
                              kind="ExternalInput").ap()
    V["d_u"] = nc.dram_tensor("wu", [C, 128, 4096], F32,
                              kind="ExternalInput").ap()
    V["d_d"] = nc.dram_tensor("wd", [M, 128, 7168], F32,
                              kind="ExternalInput").ap()
    for nm in ("wq", "wk", "wv", "wo"):
        V["d_" + nm] = nc.dram_tensor(nm, [M, 128, 4096], F32,
                                      kind="ExternalInput").ap()
    V["d_gw"] = nc.dram_tensor("gw", [128, 512], F32,
                               kind="ExternalInput").ap()
    V["d_b"] = nc.dram_tensor("bs", [128, 128], F32,
                              kind="ExternalInput").ap()
    V["d_id"] = nc.dram_tensor("ident_d", [128, 128], F32,
                               kind="ExternalInput").ap()
    V["d_or"] = nc.dram_tensor("onesr_d", [1, 128], F32,
                               kind="ExternalInput").ap()
    V["d_oc"] = nc.dram_tensor("onesc_d", [128, 1], F32,
                               kind="ExternalInput").ap()
    if DBG_X:
        V["d_dbgA"] = nc.dram_tensor("dbgA", [128, 32768], BF16,
                                     kind="ExternalOutput").ap()
        V["d_dbgB"] = nc.dram_tensor("dbgB", [128, 57344], BF16,
                                     kind="ExternalOutput").ap()
        V["d_dbgW"] = nc.dram_tensor("dbgW", [128, 128], BF16,
                                     kind="ExternalOutput").ap()
        V["d_dbgL"] = nc.dram_tensor("dbgL", [128, 128], F32,
                                     kind="ExternalOutput").ap()
    xkind = "ExternalOutput" if DBG_X else "Internal"
    V["d_x1"] = nc.dram_tensor("x1d", [128, M * NT], F32, kind=xkind).ap()
    V["d_x2"] = nc.dram_tensor("x2d", [128, M * NT], F32, kind=xkind).ap()
    V["d_qp"] = nc.dram_tensor("qpark", [128, 16384], BF16,
                               kind="Internal").ap()
    V["d_kp"] = nc.dram_tensor("kpark", [128, 16384], BF16,
                               kind="Internal").ap()

    guards = []

    def sb(name, shape, dtype):
        g = nc.sbuf_tensor(name, shape, dtype)
        t = g.__enter__()
        guards.append(g)
        return t

    def ps(name, shape, dtype):
        g = nc.psum_tensor(name, shape, dtype)
        t = g.__enter__()
        guards.append(g)
        return t

    V["regA"] = sb("regA", [128, 32768], BF16)
    V["regB"] = sb("regB", [128, 57344], BF16)
    V["wring"] = sb("wring", [128, 6144], BF16)
    V["stage"] = sb("stage", [128, 1024], F32)
    V["sc"] = sb("sc", [128, 512], F32)
    V["stagebf"] = sb("stagebf", [128, 1024], BF16)
    V["miscbf"] = sb("miscbf", [128, 1024], BF16)
    V["vpack"] = sb("vpack", [128, 256], BF16)
    V["m16"] = sb("m16", [1, 512], BF16)
    V["r16"] = sb("r16", [1, 512], BF16)
    V["wrow16"] = sb("wrow16", [1, 1024], BF16)
    V["gsc"] = sb("gsc", [128, 64], F32)
    V["ltok"] = sb("ltok", [128, 128], F32)
    V["wtok"] = sb("wtok", [128, 128], BF16)
    V["gatewT"] = sb("gatewT", [128, 512], BF16)
    V["biases"] = sb("biases", [128, 128], F32)
    V["identB"] = sb("identB", [128, 128], BF16)
    V["identF"] = sb("identF", [128, 128], F32)
    V["onesr"] = sb("onesr", [1, 128], BF16)
    V["epsc"] = sb("epsc", [128, 1], F32)
    V["onesc"] = sb("onesc", [128, 1], BF16)

    V["pb"] = [ps("pb%d" % i, [128, 512], F32) for i in range(6)]
    V["pbt"] = ps("pbt", [128, 1024], BF16)
    V["pbw"] = ps("pbw", [128, 1024], BF16)

    sems = {}
    for s in ("pe", "dve", "act", "pool", "sp", "wd0", "wd1", "wd2", "wp",
              "sd0", "sd1", "pk0", "pk1", "rl"):
        g = nc.semaphore(s + "_sem")
        sems[s] = g.__enter__()
        guards.append(g)

    blk = nc.Block()
    block = blk.__enter__()
    for mode, deco in (("pool", block.gpsimd), ("pe", block.tensor),
                       ("dve", block.vector), ("act", block.scalar),
                       ("sp", block.sync)):
        def body(eng, mode=mode):
            build_schedule(Em(nc, mode, sems), V)
        deco(body)
    blk.__exit__(None, None, None)
    for g in reversed(guards):
        g.__exit__(None, None, None)
    return nc


def _prep_weights(inp):
    g = np.ascontiguousarray(
        inp["L0_ffn_g"].reshape(C, 128, KK, 128).transpose(0, 3, 2, 1)
    ).reshape(C, 128, 4096)
    u = np.ascontiguousarray(
        inp["L0_ffn_u"].reshape(C, 128, KK, 128).transpose(0, 3, 2, 1)
    ).reshape(C, 128, 4096)
    d = np.ascontiguousarray(
        inp["L0_ffn_d"].reshape(M, 128, C, 128).transpose(0, 3, 2, 1)
    ).reshape(M, 128, 7168)
    ws = {}
    for nm, w in (("wq", inp["L0_attn_Wq"]), ("wk", inp["L0_attn_Wk"]),
                  ("wv", inp["L0_attn_Wv"]), ("wo", inp["L0_attn_Wo"])):
        ws[nm] = np.ascontiguousarray(
            w.reshape(M, 128, KK, 128).transpose(0, 3, 2, 1)
        ).reshape(M, 128, 4096)
    gw = np.ascontiguousarray(
        inp["L0_ffn_gate_w"].T.reshape(KK, 128, NE).transpose(1, 0, 2)
    ).reshape(128, 512)
    bs = np.ascontiguousarray(
        np.stack([b.reshape(M, 128).T for b in
                  (inp["L0_attn_bq"], inp["L0_attn_bk"], inp["L0_attn_bv"],
                   inp["L0_attn_bo"])], axis=1)).reshape(128, 128)
    return dict(wg=g, wu=u, wd=d, gw=gw, bs=bs, **ws)


def kernel(**inputs):
    inputs = {k: np.asarray(v, dtype=np.float32) for k, v in inputs.items()}
    w = _prep_weights(inputs)
    aux = dict(ident_d=np.eye(128, dtype=np.float32),
               onesr_d=np.ones((1, 128), np.float32),
               onesc_d=np.ones((128, 1), np.float32))
    x = inputs["L0_attn_x"][0]
    nc = build_nc()
    in_maps = []
    for c in range(NCORE):
        xc = np.ascontiguousarray(
            x[c * NT:(c + 1) * NT, :].T.reshape(M, 128, NT).transpose(1, 0, 2)
        ).reshape(128, M * NT)
        in_maps.append(dict(xin=xc, **aux, **w))
    res = run_bass_kernel_spmd(nc, in_maps, core_ids=list(range(NCORE)))
    out = np.empty((1, NCORE * NT, 4096), np.float32)
    for c in range(NCORE):
        yc = res.results[c]["y"].reshape(128, M, NT)
        out[0, c * NT:(c + 1) * NT, :] = yc.transpose(2, 1, 0).reshape(NT, 4096)
    return out


# version 05
# speedup vs baseline: 1.1345x; 1.1345x over previous
import dataclasses
import numpy as np
import concourse.bass as bass
import concourse.mybir as mybir
from concourse.bass_utils import run_bass_kernel_spmd

F32 = mybir.dt.float32
BF16 = mybir.dt.bfloat16
AF = mybir.ActivationFunctionType
ALU = mybir.AluOpType
AX = mybir.AxisListType

NCORE = 8
NT = 1024
HH = 512
M = 32
C = 56
KK = 32
NE = 16
INV_SQRT_D = float(1.0 / np.sqrt(128.0))
THR = 0.001
LN_EPS = 1e-5
PHLIM = 99
AVSTEP = 6
DBG_X = False


class Em:

    QENG = {"wd0": "pool", "wd1": "pool", "wd2": "pool", "wp": "pool",
            "sd0": "sp", "sd1": "sp", "pk0": "sp", "pk1": "sp", "rl": "sp"}

    def __init__(self, nc, mode, sems):
        self.nc = nc
        self.mode = mode
        self.sems = sems
        self.c = {k: 0 for k in
                  ("pe", "dve", "act", "pool", "sp") + tuple(self.QENG)}
        self.last = {}

    def _eng(self, name):
        return {"pe": self.nc.tensor, "dve": self.nc.vector,
                "act": self.nc.scalar, "pool": self.nc.gpsimd,
                "sp": self.nc.sync}[name]

    def op(self, eng, fn):
        if self.mode == eng:
            self.last[eng] = fn(self._eng(eng))

    def mark(self, eng):
        self.c[eng] += 1
        if self.mode == eng:
            self.last[eng].then_inc(self.sems[eng], 1)
        return self.c[eng]

    def dma(self, queue, fn):
        eng = self.QENG[queue]
        self.c[queue] += 16
        if self.mode == eng:
            fn(self._eng(eng)).then_inc(self.sems[queue], 16)
        return self.c[queue]

    def wait(self, eng, sem, val):
        if val <= 0 or eng == sem:
            return
        if self.mode == eng:
            self._eng(eng).wait_ge(self.sems[sem], val)

    def fence(self, eng):
        self.c[eng] += 1
        if self.mode == eng:
            self.last[eng].then_inc(self.sems[eng], 1)
            self._eng(eng).wait_ge(self.sems[eng], self.c[eng])
        return self.c[eng]

    def waitp(self, eng, pair):
        self.wait(eng, pair[0], pair[1])


def repl(ap, dims):
    return dataclasses.replace(ap, ap=[ap.ap[0]] + dims)


def bcast0(ap, reps):
    return dataclasses.replace(ap, ap=list(ap.ap) + [[0, reps]])


def seg32(ap):
    return repl(ap, [[32, 16], [1, 32]])


def build_schedule(em, V):
    S = {}
    regA, regB = V["regA"], V["regB"]
    stage, stagebf, miscbf = V["stage"], V["stagebf"], V["miscbf"]
    sc, gsc = V["sc"], V["gsc"]
    PB, PBT, PBW = V["pb"], V["pbt"], V["pbw"]

    NSLOT = 3
    ring = {"i": 0, "done": [], "cons": []}

    def wslab_push(src_ap):
        i = ring["i"]
        if i >= NSLOT:
            em.wait("pool", "pe", ring["cons"][i - NSLOT])
        slot = i % NSLOT
        w = src_ap.shape[1]
        dst = V["wring"][:, 2048 * slot: 2048 * slot + w]
        v = em.dma("wd%d" % slot,
                   lambda g, d=dst, s=src_ap: g.dma_start(out=d, in_=s))
        ring["done"].append(v)
        ring["cons"].append(None)
        ring["i"] += 1
        return i

    def wslab_wait(i):
        em.wait("pe", "wd%d" % (i % NSLOT), ring["done"][i])

    def wtile(i, t):
        slot = i % NSLOT
        return V["wring"][:, 2048 * slot + 128 * t: 2048 * slot + 128 * t + 128]

    def wslab_done(i):
        ring["cons"][i] = em.mark("pe")

    free = {b: ("dve", 0) for b in range(8)}

    def bank_wait(b, eng="pe"):
        em.waitp(eng, free[b])

    def bank_free(b, sem):
        free[b] = (sem, em.c[sem])

    bfbank = [PBT, PBW]
    bfg = [("dve", 0), ("dve", 0)]
    vp_free = [("pe", 0), ("pe", 0)]
    wtok_free = [("pe", 0)]
    sc_free = [("dve", 0)]

    stg = {("stage", 0): ("dve", 0), ("stage", 1): ("dve", 0),
           ("sbf", 0): ("dve", 0), ("sbf", 1): ("dve", 0),
           ("mbf", 0): ("dve", 0), ("mbf", 1): ("dve", 0)}

    def slot_wait(key, eng):
        em.waitp(eng, stg[key])

    def slot_set(key, sem):
        stg[key] = (sem, em.c[sem])

    def stg_slot(slot):
        return stage[:, 512 * slot:512 * slot + 512]

    def sbf_slot(slot):
        return stagebf[:, 512 * slot:512 * slot + 512]

    def mbf_slot(slot):
        return miscbf[:, 512 * slot:512 * slot + 512]

    def xln(k, hh):
        return regA[:, k * NT + hh * HH: k * NT + hh * HH + HH]

    def hv(c, hh):
        return regB[:, c * NT + hh * HH: c * NT + hh * HH + HH]

    em.dma("rl", lambda s: s.dma_start(out=V["biases"][:], in_=V["d_b"]))
    pre_if = em.dma("rl", lambda s: s.dma_start(out=V["identF"][:],
                                                in_=V["d_id"]))
    em.dma("wp", lambda g: g.dma_start(out=V["identB"][:], in_=V["d_id"]))
    em.dma("wp", lambda g: g.dma_start(out=V["gatewT"][:], in_=V["d_gw"]))
    em.dma("wp", lambda g: g.dma_start(out=V["onesr"][:], in_=V["d_or"]))
    pre_oc = em.dma("wp", lambda g: g.dma_start(out=V["onesc"][:],
                                                in_=V["d_oc"]))
    for eng in ("pe", "dve", "act"):
        em.wait(eng, "wp", pre_oc)
        em.wait(eng, "rl", pre_if)
    em.op("dve", lambda v: v.memset(V["epsc"][:], LN_EPS))
    eps_ready = em.mark("dve")
    em.wait("act", "dve", eps_ready)

    def stats_unit(m, hh, slot, first):
        pass

    def emit_stats_mms(m, hh, slot, first):
        em.op("pe", lambda t: t.matmul(
            PB[2 + hh][0:1, :], V["onesc"][:], sbf_slot(slot),
            start=first, stop=(m == M - 1), skip_group_check=True))
        em.op("pe", lambda t: t.matmul(
            PB[4 + hh][0:1, :], V["onesc"][:], mbf_slot(slot),
            start=first, stop=(m == M - 1), skip_group_check=True))
        em.mark("pe")
        slot_set(("sbf", slot), "pe")
        slot_set(("mbf", slot), "pe")

    def ln_stats_pass(src):
        for b in (2, 3, 4, 5):
            bank_wait(b, "pe")
        first = True
        units = [(m, hh) for m in range(M) for hh in (0, 1)]
        pend = {}

        def issue(u):
            m, hh = units[u]
            slot = u % 2
            slot_wait(("stage", slot), "sp")
            pend[u] = (em.dma("sd%d" % slot, lambda s: s.dma_start(
                out=stg_slot(slot),
                in_=src[:, m * NT + hh * HH: m * NT + hh * HH + HH])), slot)

        issue(0)
        for u, (m, hh) in enumerate(units):
            if True:
                if u + 1 < len(units):
                    issue(u + 1)
                dv, slot = pend.pop(u)
                em.wait("dve", "sd%d" % slot, dv)
                slot_wait(("sbf", slot), "dve")
                em.op("dve", lambda v: v.tensor_copy(sbf_slot(slot),
                                                     stg_slot(slot)))
                cm = em.mark("dve")
                slot_set(("stage", slot), "dve")
                em.wait("act", "dve", cm)
                slot_wait(("mbf", slot), "act")
                em.op("act", lambda a: a.activation(mbf_slot(slot),
                                                    sbf_slot(slot), AF.Square))
                sq = em.mark("act")
                em.wait("pe", "act", sq)
                emit_stats_mms(m, hh, slot, m == 0)
        S["ln_stats_done"] = em.c["pe"]

    def ln_finalize_apply(src, src_ready):
        t_mean = sc[0:1, :]
        t_var = stage[0:1, 0:512]
        t_tmp = stage[0:1, 512:1024]
        for hh in (0, 1):
            em.wait("dve", "pe", S["ln_stats_done"])
            em.waitp("dve", sc_free[0])
            slot_wait(("stage", 0), "dve")
            slot_wait(("stage", 1), "dve")
            em.op("dve", lambda v: v.tensor_scalar(
                t_mean, PB[2 + hh][0:1, :], 1.0 / 4096.0, None, ALU.mult))
            em.fence("dve")
            em.op("dve", lambda v: v.tensor_scalar(
                t_var, PB[4 + hh][0:1, :], 1.0 / 4096.0, None, ALU.mult))
            em.fence("dve")
            em.op("dve", lambda v: v.tensor_mul(t_tmp, t_mean, t_mean))
            em.fence("dve")
            em.op("dve", lambda v: v.tensor_sub(t_var, t_var, t_tmp))
            em.fence("dve")
            em.op("dve", lambda v: v.tensor_copy(V["m16"][:], t_mean))
            f1 = em.mark("dve")
            em.wait("act", "dve", f1)
            em.op("act", lambda a: a.activation(t_tmp, t_var, AF.Sqrt,
                                                bias=V["epsc"][0:1, :]))
            f2 = em.mark("act")
            em.wait("dve", "act", f2)
            em.op("dve", lambda v: v.reciprocal(t_var, t_tmp))
            em.fence("dve")
            em.op("dve", lambda v: v.tensor_copy(V["r16"][:], t_var))
            f3 = em.mark("dve")
            sc_free[0] = ("dve", em.c["dve"])
            slot_set(("stage", 0), "dve")
            slot_set(("stage", 1), "dve")
            bank_free(2 + hh, "dve")
            bank_free(4 + hh, "dve")
            em.wait("pe", "dve", f3)
            bank_wait(0, "pe")
            bank_wait(1, "pe")
            em.op("pe", lambda t: t.matmul(PB[0][:], V["onesr"][:],
                                           V["m16"][:], start=True, stop=True))
            em.op("pe", lambda t: t.matmul(PB[1][:], V["onesr"][:],
                                           V["r16"][:], start=True, stop=True))
            bc = em.mark("pe")
            em.wait("dve", "pe", bc)
            slot_wait(("sbf", 0), "dve")
            slot_wait(("sbf", 1), "dve")
            em.op("dve", lambda v: v.tensor_copy(stagebf[:, 0:512], PB[0][:]))
            em.op("dve", lambda v: v.tensor_copy(stagebf[:, 512:1024],
                                                 PB[1][:]))
            mrev = em.mark("dve")
            em.wait("pool", "dve", mrev)
            apend = {}

            def aissue(m, hh=hh):
                slot = m % 2
                slot_wait(("stage", slot), "sp")
                apend[m] = (em.dma("sd%d" % slot, lambda s: s.dma_start(
                    out=stg_slot(slot),
                    in_=src[:, m * NT + hh * HH: m * NT + hh * HH + HH])),
                    slot)

            if src_ready:
                em.wait("sp", "sd0", em.c["sd0"])
                em.wait("sp", "sd1", em.c["sd1"])
                src_ready = False
            aissue(0)
            for m in range(M):
                if m + 1 < M:
                    aissue(m + 1)
                dv, slot = apend.pop(m)
                eng = "pool" if m % 3 == 2 else "dve"
                em.wait(eng, "sd%d" % slot, dv)
                if eng == "dve":
                    em.wait("dve", "pe", bc)
                    em.op("dve", lambda v: v.tensor_sub(
                        stg_slot(slot), stg_slot(slot), PB[0][:]))
                    em.op("dve", lambda v: v.tensor_mul(
                        xln(m, hh), stg_slot(slot), PB[1][:]))
                else:
                    em.op("pool", lambda g: g.tensor_sub(
                        stg_slot(slot), stg_slot(slot), stagebf[:, 0:512]))
                    em.op("pool", lambda g: g.tensor_mul(
                        xln(m, hh), stg_slot(slot), stagebf[:, 512:1024]))
                em.mark(eng)
                slot_set(("stage", slot), eng)
            bank_free(0, "dve")
            bank_free(1, "dve")
            slot_set(("sbf", 0), "pool")
            slot_set(("sbf", 1), "pool")
        S["xln_ready"] = [("dve", em.c["dve"]), ("pool", em.c["pool"])]

    def gating():
        for pr in S["xln_ready"]:
            em.waitp("pe", pr)
        pass
        bank_wait(2, "pe")
        bank_wait(3, "pe")
        for hh in (0, 1):
            for k in range(KK):
                em.op("pe", lambda t: t.matmul(
                    PB[2 + hh][0:16, :], V["gatewT"][:, 16 * k:16 * k + 16],
                    xln(k, hh), start=(k == 0), stop=(k == KK - 1)))
        lg = em.mark("pe")
        em.wait("dve", "pe", lg)
        slot_wait(("stage", 0), "dve")
        slot_wait(("stage", 1), "dve")
        em.op("dve", lambda v: v.tensor_copy(stage[0:16, 0:512],
                                             PB[2][0:16, :]))
        em.op("dve", lambda v: v.tensor_copy(stage[0:16, 512:1024],
                                             PB[3][0:16, :]))
        lge = em.mark("dve")
        bank_free(2, "dve")
        bank_free(3, "dve")
        em.wait("pe", "dve", lge)
        bank_wait(4, "pe")
        for b in range(8):
            hh, bb = divmod(b, 4)
            em.op("pe", lambda t: t.transpose(
                PB[4][:, 16 * b:16 * b + 16],
                stage[0:16, hh * 512 + 128 * bb: hh * 512 + 128 * bb + 128],
                V["identF"][0:16, 0:16]))
        ltr = em.mark("pe")
        em.wait("dve", "pe", ltr)
        em.op("dve", lambda v: v.tensor_copy(V["ltok"][:], PB[4][:, 0:128]))
        em.mark("dve")
        bank_free(4, "dve")
        slot_set(("stage", 0), "dve")
        slot_set(("stage", 1), "dve")
        em.waitp("dve", wtok_free[0])
        for b in range(8):
            lt = V["ltok"][:, 16 * b:16 * b + 16]
            c_m1, c_m2 = gsc[:, 0:1], gsc[:, 1:2]
            c_e2, c_den, c_r = gsc[:, 2:3], gsc[:, 3:4], gsc[:, 4:5]
            r_t, r_u, r_x = gsc[:, 16:32], gsc[:, 32:48], gsc[:, 48:64]
            em.op("dve", lambda v: v.reduce_max(c_m1, lt, axis=AX.X))
            em.fence("dve")
            em.op("dve", lambda v: v.tensor_scalar(r_t, lt, c_m1, None,
                                                   ALU.subtract))
            em.fence("dve")
            em.op("dve", lambda v: v.tensor_scalar(r_u, r_t, 0.0, None,
                                                   ALU.is_ge))
            em.fence("dve")
            em.op("dve", lambda v: v.scalar_tensor_tensor(
                r_x, r_u, -1e30, r_t, ALU.mult, ALU.add))
            em.fence("dve")
            em.op("dve", lambda v: v.reduce_max(c_m2, r_x, axis=AX.X))
            g1 = em.mark("dve")
            em.wait("act", "dve", g1)
            em.op("act", lambda a: a.activation(c_e2, c_m2, AF.Exp))
            em.fence("act")
            em.op("act", lambda a: a.activation(r_x, r_t, AF.Exp))
            g2 = em.mark("act")
            em.wait("dve", "act", g2)
            em.op("dve", lambda v: v.tensor_scalar(c_den, c_e2, 1.0, None,
                                                   ALU.add))
            em.fence("dve")
            em.op("dve", lambda v: v.reciprocal(c_r, c_den))
            em.fence("dve")
            em.op("dve", lambda v: v.tensor_scalar(r_u, r_t, c_m2, None,
                                                   ALU.is_ge))
            em.fence("dve")
            em.op("dve", lambda v: v.tensor_mul(r_t, r_u, r_x))
            em.fence("dve")
            em.op("dve", lambda v: v.tensor_scalar(
                V["wtok"][:, 16 * b:16 * b + 16], r_t, c_r, 0.5,
                ALU.mult, ALU.mult))
            em.mark("dve")
        S["wtok_ready"] = em.c["dve"]

    fin_at = {}
    for e in range(NE):
        fin_at.setdefault((448 * e + 447) // 128, []).append(e)

    def expert_spans(e):
        lo, hi = 448 * e, 448 * (e + 1)
        spans = []
        c0 = lo // 128
        if lo % 128:
            spans.append((slice(64, 128), c0, 1))
            c0 += 1
        nfull = hi // 128 - c0
        if nfull:
            spans.append((slice(0, 128), c0, nfull))
        if hi % 128:
            spans.append((slice(0, 64), hi // 128, 1))
        return spans

    def hspan_ap(psl, c0, hh, nch):
        base = regB[psl, c0 * NT + hh * HH: c0 * NT + hh * HH + HH]
        return repl(base, [[NT, nch], [1, HH]])

    def wspan_ap(psl, nch, hh2):
        return repl(PB[2 + hh2][psl, :], [[0, nch], [1, HH]])

    def ffn_gu(wg_src, wu_src):
        hmul = {}
        for c in range(C):
            sg0 = wslab_push(wg_src[c, :, 0:2048])
            sg1 = wslab_push(wg_src[c, :, 2048:4096])
            su0 = wslab_push(wu_src[c, :, 0:2048])
            gb0, gb1 = PB[0], PB[1]
            for pr in S["xln_ready"]:
                em.waitp("pe", pr)
            bank_wait(0, "pe")
            bank_wait(1, "pe")
            for k in range(KK):
                slab = sg0 if k < 16 else sg1
                if k in (0, 16):
                    wslab_wait(slab)
                em.op("pe", lambda t: t.matmul(
                    gb0[:], wtile(slab, k % 16), xln(k, 0),
                    start=(k == 0), stop=(k == KK - 1)))
                em.op("pe", lambda t: t.matmul(
                    gb1[:], wtile(slab, k % 16), xln(k, 1),
                    start=(k == 0), stop=(k == KK - 1)))
                if k == 15:
                    wslab_done(sg0)
            wslab_done(sg1)
            su1 = wslab_push(wu_src[c, :, 2048:4096])
            gdone = em.c["pe"]
            bank_wait(4, "pe")
            bank_wait(5, "pe")
            for k in range(KK):
                slab = su0 if k < 16 else su1
                if k in (0, 16):
                    wslab_wait(slab)
                em.op("pe", lambda t: t.matmul(
                    PB[4][:], wtile(slab, k % 16), xln(k, 0),
                    start=(k == 0), stop=(k == KK - 1)))
                em.op("pe", lambda t: t.matmul(
                    PB[5][:], wtile(slab, k % 16), xln(k, 1),
                    start=(k == 0), stop=(k == KK - 1)))
                if k == 15:
                    wslab_done(su0)
            wslab_done(su1)
            udone = em.c["pe"]
            for hh in (0, 1):
                gb = gb0 if hh == 0 else gb1
                ub = PB[4 + hh]
                em.wait("act", "pe", gdone)
                slot_wait(("mbf", hh), "act")
                em.op("act", lambda a: a.activation(mbf_slot(hh), gb[:],
                                                    AF.Silu))
                sm = em.mark("act")
                bank_free(hh, "act")
                em.wait("dve", "act", sm)
                em.wait("dve", "pe", udone)
                em.op("dve", lambda v: v.tensor_mul(hv(c, hh), mbf_slot(hh),
                                                    ub[:]))
                hmul[(c, hh)] = em.mark("dve")
                slot_set(("mbf", hh), "dve")
                bank_free(4 + hh, "dve")
            for e in fin_at.get(c, []):
                em.wait("pe", "dve", S["wtok_ready"])
                em.waitp("pe", bfg[0])
                em.waitp("pe", bfg[1])
                for b in range(8):
                    em.op("pe", lambda t: t.transpose(
                        PBT[0:1, 128 * b:128 * b + 128],
                        V["wtok"][:, 16 * b + e:16 * b + e + 1],
                        V["identB"][:]))
                wtr = em.mark("pe")
                em.wait("dve", "pe", wtr)
                em.op("dve", lambda v: v.tensor_copy(V["wrow16"][0:1, 0:512],
                                                     PBT[0:1, 0:512]))
                em.op("dve", lambda v: v.tensor_copy(
                    V["wrow16"][0:1, 512:1024], PBT[0:1, 512:1024]))
                wre = em.mark("dve")
                bfg[0] = ("dve", em.c["dve"])
                em.wait("pe", "dve", wre)
                wbs = []
                for hh2 in (0, 1):
                    bank_wait(2 + hh2, "pe")
                    em.op("pe", lambda t: t.matmul(
                        PB[2 + hh2][:], V["onesr"][:],
                        V["wrow16"][0:1, 512 * hh2:512 * hh2 + 512],
                        start=True, stop=True))
                    wbs.append(em.mark("pe"))
                for hh2 in (0, 1):
                    em.wait("dve", "pe", wbs[hh2])
                    for (psl, c0, nch) in expert_spans(e):
                        em.op("dve", lambda v: v.tensor_mul(
                            hspan_ap(psl, c0, hh2, nch),
                            hspan_ap(psl, c0, hh2, nch),
                            wspan_ap(psl, nch, hh2)))
                    em.mark("dve")
                    bank_free(2 + hh2, "dve")
        wtok_free[0] = ("pe", em.c["pe"])
        S["h_ready"] = em.c["dve"]

    def ffn_down(wd_src, x_src, dst_dram, stats):
        if stats:
            for b in (2, 3, 4, 5):
                bank_wait(b, "pe")
        wv = 0
        dpend = {}

        def dissue(u):
            m2, hh2 = divmod(u, 2)
            slot = u % 2
            slot_wait(("stage", slot), "sp")
            dpend[u] = (em.dma("sd%d" % slot, lambda s: s.dma_start(
                out=stg_slot(slot),
                in_=x_src[:, m2 * NT + hh2 * HH: m2 * NT + hh2 * HH + HH])),
                slot)

        for m in range(M):
            if stats:
                i0, i1 = 0, 1
            else:
                i0, i1 = 2 * (m % 2), 2 * (m % 2) + 1
            b0, b1 = PB[i0], PB[i1]
            sl = [wslab_push(wd_src[m, :, 2048 * q: 2048 * q + 2048])
                  for q in range(3)]
            em.wait("pe", "dve", S["h_ready"])
            bank_wait(i0, "pe")
            bank_wait(i1, "pe")
            for c in range(C):
                q, t = divmod(c, 16)
                if c == 16:
                    sl.append(wslab_push(wd_src[m, :, 6144:7168]))
                if t == 0:
                    wslab_wait(sl[q])
                em.op("pe", lambda te: te.matmul(
                    b0[:], wtile(sl[q], t), hv(c, 0),
                    start=(c == 0), stop=(c == C - 1)))
                em.op("pe", lambda te: te.matmul(
                    b1[:], wtile(sl[q], t), hv(c, 1),
                    start=(c == 0), stop=(c == C - 1)))
                if t == 15 or c == C - 1:
                    wslab_done(sl[q])
            ddone = em.c["pe"]
            for hh in (0, 1):
                bank = b0 if hh == 0 else b1
                u = 2 * m + hh
                if u == 0:
                    dissue(0)
                if u + 1 < 2 * M:
                    dissue(u + 1)
                dv, slot = dpend.pop(u)
                em.wait("dve", "sd%d" % slot, dv)
                em.wait("dve", "pe", ddone)
                em.op("dve", lambda v: v.tensor_add(stg_slot(slot),
                                                    stg_slot(slot), bank[:]))
                if stats:
                    slot_wait(("sbf", slot), "dve")
                    em.op("dve", lambda v: v.tensor_copy(sbf_slot(slot),
                                                         stg_slot(slot)))
                ev = em.mark("dve")
                bank_free(i0 + hh, "dve")
                em.wait("sp", "dve", ev)
                wv = em.dma("sd%d" % slot, lambda s: s.dma_start(
                    out=dst_dram[:, m * NT + hh * HH: m * NT + hh * HH + HH],
                    in_=stg_slot(slot)))
                slot_set(("stage", slot), "sd%d" % slot)
                if stats:
                    em.wait("act", "dve", ev)
                    slot_wait(("mbf", slot), "act")
                    em.op("act", lambda a: a.activation(
                        mbf_slot(slot), sbf_slot(slot), AF.Square))
                    sq = em.mark("act")
                    em.wait("pe", "act", sq)
                    emit_stats_mms(m, hh, slot, m == 0)
        if stats:
            S["ln_stats_done"] = em.c["pe"]
        return wv

    def proj(w_src, kind, rhs_fn, bias_idx, x_src=None, dst_dram=None,
             park=None, stats=False):
        evmarks, parkw = [], []
        wrlast = 0
        opend = {}

        def oissue(u):
            m2, hh2 = divmod(u, 2)
            slot = u % 2
            slot_wait(("stage", slot), "sp")
            opend[u] = (em.dma("sd%d" % slot, lambda s: s.dma_start(
                out=stg_slot(slot),
                in_=x_src[:, m2 * NT + hh2 * HH: m2 * NT + hh2 * HH + HH])),
                slot)

        if stats:
            for b in (2, 3, 4, 5):
                bank_wait(b, "pe")
        for m in range(M):
            s0 = wslab_push(w_src[m, :, 0:2048])
            s1 = wslab_push(w_src[m, :, 2048:4096])
            if stats:
                i0, i1 = 0, 1
            else:
                i0, i1 = 2 * (m % 2), 2 * (m % 2) + 1
            b0, b1 = PB[i0], PB[i1]
            for pr in (S["rhs_ready"] if isinstance(S["rhs_ready"], list)
                       else [S["rhs_ready"]]):
                em.waitp("pe", pr)
            bank_wait(i0, "pe")
            bank_wait(i1, "pe")
            for k in range(KK):
                slab = s0 if k < 16 else s1
                if k in (0, 16):
                    wslab_wait(slab)
                em.op("pe", lambda t: t.matmul(
                    b0[:], wtile(slab, k % 16), rhs_fn(k, 0),
                    start=(k == 0), stop=(k == KK - 1)))
                em.op("pe", lambda t: t.matmul(
                    b1[:], wtile(slab, k % 16), rhs_fn(k, 1),
                    start=(k == 0), stop=(k == KK - 1)))
                if k == 15:
                    wslab_done(s0)
            wslab_done(s1)
            pdone = em.c["pe"]
            bias = V["biases"][:, 32 * bias_idx + m: 32 * bias_idx + m + 1]
            if kind in ("q", "k"):
                base = 0 if kind == "q" else 16384
                em.wait("act", "pe", pdone)
                em.op("act", lambda a: a.activation(
                    regB[:, base + m * HH: base + m * HH + HH], b0[:],
                    AF.Identity, bias=bias))
                e0 = em.mark("act")
                bank_free(i0, "act")
                slot = m % 2
                slot_wait(("mbf", slot), "act")
                em.op("act", lambda a: a.activation(mbf_slot(slot), b1[:],
                                                    AF.Identity, bias=bias))
                e1 = em.mark("act")
                bank_free(i1, "act")
                em.wait("sp", "act", e1)
                pw = em.dma("pk%d" % slot, lambda s: s.dma_start(
                    out=park[:, m * HH: m * HH + HH], in_=mbf_slot(slot)))
                parkw.append(pw)
                slot_set(("mbf", slot), "pk%d" % slot)
                evmarks.append(e1)
            elif kind == "v":
                em.wait("act", "pe", pdone)
                if m == 0:
                    em.wait("act", "pe", S["scores_h1_done"])
                for hh in (0, 1):
                    bank = b0 if hh == 0 else b1
                    base = regB[:, 16384 + hh * 16384 + m:
                                16384 + hh * 16384 + m + 1]
                    dst = repl(base, [[32, HH]])
                    em.op("act", lambda a: a.activation(dst, bank[:], AF.Identity,
                                                        bias=bias))
                    evmarks.append(em.mark("act"))
                    bank_free(i0 + hh, "act")
            elif kind == "o":
                for hh in (0, 1):
                    bank = b0 if hh == 0 else b1
                    slot = (2 * m + hh) % 2
                    em.wait("act", "pe", pdone)
                    em.waitp("act", sc_free[0])
                    em.op("act", lambda a: a.activation(sc[:, :], bank[:],
                                                        AF.Identity, bias=bias))
                    ab = em.mark("act")
                    bank_free(i0 + hh, "act")
                    u = 2 * m + hh
                    if u == 0:
                        oissue(0)
                    if u + 1 < 2 * M:
                        oissue(u + 1)
                    dv, slot = opend.pop(u)
                    em.wait("dve", "sd%d" % slot, dv)
                    em.wait("dve", "act", ab)
                    em.op("dve", lambda v: v.tensor_add(stg_slot(slot),
                                                        stg_slot(slot),
                                                        sc[:, :]))
                    if stats:
                        slot_wait(("sbf", slot), "dve")
                        em.op("dve", lambda v: v.tensor_copy(sbf_slot(slot),
                                                             stg_slot(slot)))
                    ev = em.mark("dve")
                    sc_free[0] = ("dve", em.c["dve"])
                    em.wait("sp", "dve", ev)
                    wrlast = em.dma("sd%d" % slot, lambda s: s.dma_start(
                        out=dst_dram[:, m * NT + hh * HH:
                                     m * NT + hh * HH + HH],
                        in_=stg_slot(slot)))
                    slot_set(("stage", slot), "sd%d" % slot)
                    if stats:
                        em.wait("act", "dve", ev)
                        slot_wait(("mbf", slot), "act")
                        em.op("act", lambda a: a.activation(
                            mbf_slot(slot), sbf_slot(slot), AF.Square))
                        sq = em.mark("act")
                        em.wait("pe", "act", sq)
                        emit_stats_mms(m, hh, slot, m == 0)
        if stats:
            S["ln_stats_done"] = em.c["pe"]
        return evmarks, parkw, wrlast

    def scores_half(half, qk_ready):
        for kb in range(8):
            t0 = 64 * kb
            bank = PB[kb % 2]
            bank_wait(kb % 2, "pe")
            em.waitp("pe", qk_ready)
            for s in range(64):
                a = s % 4
                b = s // 4
                tl = t0 + s
                qs = regB[:, tl:16384:HH]
                ks = regB[:, 16384 + tl:32768:HH]
                em.op("pe", lambda t: t.matmul(
                    bank[32 * a:32 * a + 32, 32 * b:32 * b + 32], qs, ks,
                    start=True, stop=True,
                    tile_position=(0, 32 * a), skip_group_check=True))
            mm = em.mark("pe")
            em.wait("dve", "pe", mm)
            em.waitp("dve", sc_free[0])
            em.op("dve", lambda v: v.tensor_scalar(
                sc[:, :], bank[:], THR / INV_SQRT_D, 100.0, ALU.is_gt,
                ALU.mult))
            em.fence("dve")
            em.op("dve", lambda v: v.scalar_tensor_tensor(
                sc[:, :], bank[:], INV_SQRT_D, sc[:, :], ALU.mult, ALU.add))
            em.fence("dve")
            em.op("dve", lambda v: v.reduce_max(gsc[:, 0:16],
                                                seg32(sc[:, :]), axis=AX.X))
            em.fence("dve")
            em.op("dve", lambda v: v.tensor_sub(
                seg32(sc[:, :]), seg32(sc[:, :]), bcast0(gsc[:, 0:16], 32)))
            d1 = em.mark("dve")
            bank_free(kb % 2, "dve")
            em.wait("act", "dve", d1)
            slot_wait(("sbf", 0), "act")
            em.op("act", lambda a: a.activation(stagebf[:, 0:512], sc[:, :],
                                                AF.Exp))
            ex = em.mark("act")
            sc_free[0] = ("act", em.c["act"])
            em.wait("dve", "act", ex)
            slot_wait(("sbf", 1), "dve")
            em.op("dve", lambda v: v.reduce_sum(
                gsc[:, 16:32], seg32(stagebf[:, 0:512]), axis=AX.X))
            em.fence("dve")
            em.op("dve", lambda v: v.reciprocal(gsc[:, 32:48], gsc[:, 16:32]))
            em.fence("dve")
            em.op("dve", lambda v: v.tensor_mul(
                seg32(stagebf[:, 512:1024]), seg32(stagebf[:, 0:512]),
                bcast0(gsc[:, 32:48], 32)))
            em.fence("dve")
            atcol = 49152 + (8 * half + kb) * 512
            for j in range(16):
                em.op("dve", lambda v: v.transpose(
                    regB[:, atcol + 32 * j: atcol + 32 * j + 32],
                    stagebf[:, 512 + 32 * j: 512 + 32 * j + 32]))
            em.mark("dve")
            slot_set(("sbf", 0), "dve")
            slot_set(("sbf", 1), "dve")
        S["scores_done_%d" % half] = em.c["pe"]
        S["attnT_done_%d" % half] = em.c["dve"]

    def av_phase():
        em.wait("pe", "act", S["v_evac_done"])
        em.wait("pe", "dve", S["attnT_done_1"])
        for grp in range(64):
            t0 = 16 * grp
            avb = 4 + grp % 2
            avmm = 0
            for i in range(4):
                bk = i % 2
                em.waitp("pe", bfg[bk])
                em.op("pe", lambda t: t.transpose(
                    bfbank[bk][:, 0:128],
                    regB[:, 16384 + (t0 + 4 * i) * 32:
                         16384 + (t0 + 4 * i) * 32 + 128],
                    V["identB"][:]))
                vtr = em.mark("pe")
                em.wait("act", "pe", vtr)
                vslot = i % 2
                em.waitp("act", vp_free[vslot])
                em.op("act", lambda a: a.activation(
                    V["vpack"][:, 128 * vslot:128 * vslot + 128],
                    bfbank[bk][:, 0:128], AF.Identity))
                vpe = em.mark("act")
                bfg[bk] = ("act", em.c["act"])
                em.wait("pe", "act", vpe)
                if i == 0:
                    bank_wait(avb, "pe")
                for j in range(4):
                    tg = t0 + 4 * i + j
                    kbg = tg // 64
                    s64 = tg % 64
                    acol = 49152 + kbg * 512 + 32 * (s64 // 4)
                    em.op("pe", lambda t: t.matmul(
                        PB[avb][32 * j:32 * j + 32, 128 * i:128 * i + 128],
                        regB[32 * j:32 * j + 32, acol:acol + 32],
                        V["vpack"][32 * j:32 * j + 32,
                                   128 * vslot:128 * vslot + 128],
                        start=True, stop=True,
                        tile_position=(32 * j, 32 * j),
                        skip_group_check=True))
                avmm = em.mark("pe")
                vp_free[vslot] = ("pe", em.c["pe"])
            em.wait("dve", "pe", avmm)
            slot_wait(("mbf", 0), "dve")
            em.op("dve", lambda v: v.tensor_copy(miscbf[:, 0:512],
                                                 PB[avb][:]))
            ae = em.mark("dve")
            bank_free(avb, "dve")
            for i in range(4):
                bk = i % 2
                em.waitp("pe", bfg[bk])
                em.wait("pe", "dve", ae)
                em.op("pe", lambda t: t.transpose(
                    bfbank[bk][:, 128:256],
                    miscbf[:, 128 * i:128 * i + 128], V["identB"][:]))
                otr = em.mark("pe")
                em.wait("dve", "pe", otr)
                srcap = repl(bfbank[bk][:, 128:129], [[32, 4], [1, 32]])
                dst = repl(regA[:, t0 + 4 * i: t0 + 4 * i + 1],
                           [[1, 4], [NT, 32]])
                em.op("dve", lambda v: v.tensor_copy(dst, srcap))
                em.mark("dve")
                bfg[bk] = ("dve", em.c["dve"])
            slot_set(("mbf", 0), "dve")
        S["ao_ready"] = em.c["dve"]

    def fin():
        if DBG_X:
            em.wait("sp", "pe", em.c["pe"])
            em.wait("sp", "dve", em.c["dve"])
            em.wait("sp", "act", em.c["act"])
            em.dma("rl", lambda s: s.dma_start(out=V["d_dbgA"], in_=regA[:, :]))
            em.dma("rl", lambda s: s.dma_start(out=V["d_dbgB"], in_=regB[:, :]))
            em.dma("rl", lambda s: s.dma_start(out=V["d_dbgW"],
                                               in_=V["wtok"][:, :]))
            em.dma("rl", lambda s: s.dma_start(out=V["d_dbgL"],
                                               in_=V["ltok"][:, :]))
        for q in Em.QENG:
            em.wait("sp", q, em.c[q])

    ln_stats_pass(V["d_x"])
    if PHLIM <= 0:
        return fin()
    ln_finalize_apply(V["d_x"], False)
    if PHLIM <= 1:
        return fin()
    gating()
    if PHLIM <= 2:
        return fin()
    ffn_gu(V["d_g"], V["d_u"])
    if PHLIM <= 3:
        return fin()
    x1w = ffn_down(V["d_d"], V["d_x"], V["d_x1"], stats=True)
    if PHLIM <= 4:
        return fin()
    ln_finalize_apply(V["d_x1"], True)
    S["rhs_ready"] = S["xln_ready"]
    if PHLIM <= 5:
        return fin()
    qe, qpw, _ = proj(V["d_wq"], "q", xln, 0, park=V["d_qp"])
    ke, kpw, _ = proj(V["d_wk"], "k", xln, 1, park=V["d_kp"])
    if PHLIM <= 6:
        return fin()
    scores_half(0, ("act", ke[-1]))
    if PHLIM <= 7:
        return fin()
    em.wait("sp", "pk0", em.c["pk0"])
    em.wait("sp", "pk1", em.c["pk1"])
    em.wait("sp", "pe", S["scores_done_0"])
    em.dma("rl", lambda s: s.dma_start(out=regB[:, 0:16384],
                                       in_=V["d_qp"][:, :]))
    rl2 = em.dma("rl", lambda s: s.dma_start(out=regB[:, 16384:32768],
                                             in_=V["d_kp"][:, :]))
    scores_half(1, ("rl", rl2))
    S["scores_h1_done"] = S["scores_done_1"]
    if PHLIM <= 8:
        return fin()
    ve, _, _ = proj(V["d_wv"], "v", xln, 2)
    S["v_evac_done"] = ve[-1]
    if PHLIM <= 9:
        return fin()
    av_phase()
    S["rhs_ready"] = [("dve", S["ao_ready"])]
    if PHLIM <= 10:
        return fin()

    def ao_rhs(k, hh):
        return regA[:, k * NT + hh * HH: k * NT + hh * HH + HH]

    _, _, x2w = proj(V["d_wo"], "o", ao_rhs, 3, x_src=V["d_x1"],
                     dst_dram=V["d_x2"], stats=True)
    if PHLIM <= 11:
        return fin()
    ln_finalize_apply(V["d_x2"], True)
    if PHLIM <= 12:
        return fin()
    gating()
    ffn_gu(V["d_g"], V["d_u"])
    if PHLIM <= 13:
        return fin()
    ffn_down(V["d_d"], V["d_x2"], V["d_y"], stats=False)
    fin()


def build_nc():
    nc = bass.Bass("TRN2", target_bir_lowering=False,
                   detect_race_conditions=False)
    V = {}
    V["d_x"] = nc.dram_tensor("xin", [128, M * NT], F32,
                              kind="ExternalInput").ap()
    V["d_y"] = nc.dram_tensor("y", [128, M * NT], F32,
                              kind="ExternalOutput").ap()
    V["d_g"] = nc.dram_tensor("wg", [C, 128, 4096], F32,
                              kind="ExternalInput").ap()
    V["d_u"] = nc.dram_tensor("wu", [C, 128, 4096], F32,
                              kind="ExternalInput").ap()
    V["d_d"] = nc.dram_tensor("wd", [M, 128, 7168], F32,
                              kind="ExternalInput").ap()
    for nm in ("wq", "wk", "wv", "wo"):
        V["d_" + nm] = nc.dram_tensor(nm, [M, 128, 4096], F32,
                                      kind="ExternalInput").ap()
    V["d_gw"] = nc.dram_tensor("gw", [128, 512], F32,
                               kind="ExternalInput").ap()
    V["d_b"] = nc.dram_tensor("bs", [128, 128], F32,
                              kind="ExternalInput").ap()
    V["d_id"] = nc.dram_tensor("ident_d", [128, 128], F32,
                               kind="ExternalInput").ap()
    V["d_or"] = nc.dram_tensor("onesr_d", [1, 128], F32,
                               kind="ExternalInput").ap()
    V["d_oc"] = nc.dram_tensor("onesc_d", [128, 1], F32,
                               kind="ExternalInput").ap()
    if DBG_X:
        V["d_dbgA"] = nc.dram_tensor("dbgA", [128, 32768], BF16,
                                     kind="ExternalOutput").ap()
        V["d_dbgB"] = nc.dram_tensor("dbgB", [128, 57344], BF16,
                                     kind="ExternalOutput").ap()
        V["d_dbgW"] = nc.dram_tensor("dbgW", [128, 128], BF16,
                                     kind="ExternalOutput").ap()
        V["d_dbgL"] = nc.dram_tensor("dbgL", [128, 128], F32,
                                     kind="ExternalOutput").ap()
    xkind = "ExternalOutput" if DBG_X else "Internal"
    V["d_x1"] = nc.dram_tensor("x1d", [128, M * NT], F32, kind=xkind).ap()
    V["d_x2"] = nc.dram_tensor("x2d", [128, M * NT], F32, kind=xkind).ap()
    V["d_qp"] = nc.dram_tensor("qpark", [128, 16384], BF16,
                               kind="Internal").ap()
    V["d_kp"] = nc.dram_tensor("kpark", [128, 16384], BF16,
                               kind="Internal").ap()

    guards = []

    def sb(name, shape, dtype):
        g = nc.sbuf_tensor(name, shape, dtype)
        t = g.__enter__()
        guards.append(g)
        return t

    def ps(name, shape, dtype):
        g = nc.psum_tensor(name, shape, dtype)
        t = g.__enter__()
        guards.append(g)
        return t

    V["regA"] = sb("regA", [128, 32768], BF16)
    V["regB"] = sb("regB", [128, 57344], BF16)
    V["wring"] = sb("wring", [128, 6144], BF16)
    V["stage"] = sb("stage", [128, 1024], F32)
    V["sc"] = sb("sc", [128, 512], F32)
    V["stagebf"] = sb("stagebf", [128, 1024], BF16)
    V["miscbf"] = sb("miscbf", [128, 1024], BF16)
    V["vpack"] = sb("vpack", [128, 256], BF16)
    V["m16"] = sb("m16", [1, 512], BF16)
    V["r16"] = sb("r16", [1, 512], BF16)
    V["wrow16"] = sb("wrow16", [1, 1024], BF16)
    V["gsc"] = sb("gsc", [128, 64], F32)
    V["ltok"] = sb("ltok", [128, 128], F32)
    V["wtok"] = sb("wtok", [128, 128], BF16)
    V["gatewT"] = sb("gatewT", [128, 512], BF16)
    V["biases"] = sb("biases", [128, 128], F32)
    V["identB"] = sb("identB", [128, 128], BF16)
    V["identF"] = sb("identF", [128, 128], F32)
    V["onesr"] = sb("onesr", [1, 128], BF16)
    V["epsc"] = sb("epsc", [128, 1], F32)
    V["onesc"] = sb("onesc", [128, 1], BF16)

    V["pb"] = [ps("pb%d" % i, [128, 512], F32) for i in range(6)]
    V["pbt"] = ps("pbt", [128, 1024], BF16)
    V["pbw"] = ps("pbw", [128, 1024], BF16)

    sems = {}
    for s in ("pe", "dve", "act", "pool", "sp", "wd0", "wd1", "wd2", "wp",
              "sd0", "sd1", "pk0", "pk1", "rl"):
        g = nc.semaphore(s + "_sem")
        sems[s] = g.__enter__()
        guards.append(g)

    blk = nc.Block()
    block = blk.__enter__()
    for mode, deco in (("pool", block.gpsimd), ("pe", block.tensor),
                       ("dve", block.vector), ("act", block.scalar),
                       ("sp", block.sync)):
        def body(eng, mode=mode):
            build_schedule(Em(nc, mode, sems), V)
        deco(body)
    blk.__exit__(None, None, None)
    for g in reversed(guards):
        g.__exit__(None, None, None)
    return nc


def _prep_weights(inp):
    g = np.ascontiguousarray(
        inp["L0_ffn_g"].reshape(C, 128, KK, 128).transpose(0, 3, 2, 1)
    ).reshape(C, 128, 4096)
    u = np.ascontiguousarray(
        inp["L0_ffn_u"].reshape(C, 128, KK, 128).transpose(0, 3, 2, 1)
    ).reshape(C, 128, 4096)
    d = np.ascontiguousarray(
        inp["L0_ffn_d"].reshape(M, 128, C, 128).transpose(0, 3, 2, 1)
    ).reshape(M, 128, 7168)
    ws = {}
    for nm, w in (("wq", inp["L0_attn_Wq"]), ("wk", inp["L0_attn_Wk"]),
                  ("wv", inp["L0_attn_Wv"]), ("wo", inp["L0_attn_Wo"])):
        ws[nm] = np.ascontiguousarray(
            w.reshape(M, 128, KK, 128).transpose(0, 3, 2, 1)
        ).reshape(M, 128, 4096)
    gw = np.ascontiguousarray(
        inp["L0_ffn_gate_w"].T.reshape(KK, 128, NE).transpose(1, 0, 2)
    ).reshape(128, 512)
    bs = np.ascontiguousarray(
        np.stack([b.reshape(M, 128).T for b in
                  (inp["L0_attn_bq"], inp["L0_attn_bk"], inp["L0_attn_bv"],
                   inp["L0_attn_bo"])], axis=1)).reshape(128, 128)
    return dict(wg=g, wu=u, wd=d, gw=gw, bs=bs, **ws)


def kernel(**inputs):
    inputs = {k: np.asarray(v, dtype=np.float32) for k, v in inputs.items()}
    w = _prep_weights(inputs)
    aux = dict(ident_d=np.eye(128, dtype=np.float32),
               onesr_d=np.ones((1, 128), np.float32),
               onesc_d=np.ones((128, 1), np.float32))
    x = inputs["L0_attn_x"][0]
    nc = build_nc()
    in_maps = []
    for c in range(NCORE):
        xc = np.ascontiguousarray(
            x[c * NT:(c + 1) * NT, :].T.reshape(M, 128, NT).transpose(1, 0, 2)
        ).reshape(128, M * NT)
        in_maps.append(dict(xin=xc, **aux, **w))
    res = run_bass_kernel_spmd(nc, in_maps, core_ids=list(range(NCORE)))
    out = np.empty((1, NCORE * NT, 4096), np.float32)
    for c in range(NCORE):
        yc = res.results[c]["y"].reshape(128, M, NT)
        out[0, c * NT:(c + 1) * NT, :] = yc.transpose(2, 1, 0).reshape(NT, 4096)
    return out


# version 07
# speedup vs baseline: 1.1484x; 1.0122x over previous
import dataclasses
import numpy as np
import concourse.bass as bass
import concourse.mybir as mybir
from concourse.bass_utils import run_bass_kernel_spmd

F32 = mybir.dt.float32
BF16 = mybir.dt.bfloat16
AF = mybir.ActivationFunctionType
ALU = mybir.AluOpType
AX = mybir.AxisListType

NCORE = 8
NT = 1024
HH = 512
M = 32
C = 56
KK = 32
NE = 16
INV_SQRT_D = float(1.0 / np.sqrt(128.0))
THR = 0.001
LN_EPS = 1e-5
PHLIM = 99
AVSTEP = 6
DBG_X = False


class Em:

    QENG = {"wd0": "pool", "wd1": "pool", "wd2": "pool", "wp": "pool",
            "sd0": "sp", "sd1": "sp", "sd2": "sp", "sd3": "sp",
            "pk0": "sp", "pk1": "sp", "rl": "sp"}

    def __init__(self, nc, mode, sems):
        self.nc = nc
        self.mode = mode
        self.sems = sems
        self.c = {k: 0 for k in
                  ("pe", "dve", "act", "pool", "sp") + tuple(self.QENG)}
        self.last = {}

    def _eng(self, name):
        return {"pe": self.nc.tensor, "dve": self.nc.vector,
                "act": self.nc.scalar, "pool": self.nc.gpsimd,
                "sp": self.nc.sync}[name]

    def op(self, eng, fn):
        if self.mode == eng:
            self.last[eng] = fn(self._eng(eng))

    def mark(self, eng):
        self.c[eng] += 1
        if self.mode == eng:
            self.last[eng].then_inc(self.sems[eng], 1)
        return self.c[eng]

    def dma(self, queue, fn):
        eng = self.QENG[queue]
        self.c[queue] += 16
        if self.mode == eng:
            fn(self._eng(eng)).then_inc(self.sems[queue], 16)
        return self.c[queue]

    def wait(self, eng, sem, val):
        if val <= 0 or eng == sem:
            return
        if self.mode == eng:
            self._eng(eng).wait_ge(self.sems[sem], val)

    def fence(self, eng):
        self.c[eng] += 1
        if self.mode == eng:
            self.last[eng].then_inc(self.sems[eng], 1)
            self._eng(eng).wait_ge(self.sems[eng], self.c[eng])
        return self.c[eng]

    def waitp(self, eng, pair):
        self.wait(eng, pair[0], pair[1])


def repl(ap, dims):
    return dataclasses.replace(ap, ap=[ap.ap[0]] + dims)


def bcast0(ap, reps):
    return dataclasses.replace(ap, ap=list(ap.ap) + [[0, reps]])


def seg32(ap):
    return repl(ap, [[32, 16], [1, 32]])


def build_schedule(em, V):
    S = {}
    regA, regB = V["regA"], V["regB"]
    stage, stagebf, miscbf = V["stage"], V["stagebf"], V["miscbf"]
    sc, gsc = V["sc"], V["gsc"]
    PB, PBT, PBW = V["pb"], V["pbt"], V["pbw"]

    NSLOT = 3
    ring = {"i": 0, "done": [], "cons": []}

    def wslab_push(src_ap):
        i = ring["i"]
        if i >= NSLOT:
            em.wait("pool", "pe", ring["cons"][i - NSLOT])
        slot = i % NSLOT
        w = src_ap.shape[1]
        dst = V["wring"][:, 2048 * slot: 2048 * slot + w]
        v = em.dma("wd%d" % slot,
                   lambda g, d=dst, s=src_ap: g.dma_start(out=d, in_=s))
        ring["done"].append(v)
        ring["cons"].append(None)
        ring["i"] += 1
        return i

    def wslab_wait(i):
        em.wait("pe", "wd%d" % (i % NSLOT), ring["done"][i])

    def wtile(i, t):
        slot = i % NSLOT
        return V["wring"][:, 2048 * slot + 128 * t: 2048 * slot + 128 * t + 128]

    def wslab_done(i):
        ring["cons"][i] = em.mark("pe")

    free = {b: ("dve", 0) for b in range(8)}

    def bank_wait(b, eng="pe"):
        em.waitp(eng, free[b])

    def bank_free(b, sem):
        free[b] = (sem, em.c[sem])

    bfbank = [PBT, PBW]
    bfg = [("dve", 0), ("dve", 0)]
    vp_free = [("pe", 0), ("pe", 0)]
    wtok_free = [("pe", 0)]
    sc_free = [("dve", 0)]

    stg = {("stage", 0): ("dve", 0), ("stage", 1): ("dve", 0),
           ("stage", 2): ("dve", 0),
           ("sbf", 0): ("dve", 0), ("sbf", 1): ("dve", 0),
           ("mbf", 0): ("dve", 0), ("mbf", 1): ("dve", 0)}

    def slot_wait(key, eng):
        em.waitp(eng, stg[key])

    def slot_set(key, sem):
        stg[key] = (sem, em.c[sem])

    def stg_slot(slot):
        return stage[:, 512 * slot:512 * slot + 512]

    def sbf_slot(slot):
        return stagebf[:, 512 * slot:512 * slot + 512]

    def mbf_slot(slot):
        return miscbf[:, 512 * slot:512 * slot + 512]

    def xln(k, hh):
        return regA[:, k * NT + hh * HH: k * NT + hh * HH + HH]

    def hv(c, hh):
        return regB[:, c * NT + hh * HH: c * NT + hh * HH + HH]

    em.dma("rl", lambda s: s.dma_start(out=V["biases"][:], in_=V["d_b"]))
    pre_if = em.dma("rl", lambda s: s.dma_start(out=V["identF"][:],
                                                in_=V["d_id"][0:16, 0:16]))
    em.dma("wp", lambda g: g.dma_start(out=V["identB"][:], in_=V["d_id"]))
    em.dma("wp", lambda g: g.dma_start(out=V["gatewT"][:], in_=V["d_gw"]))
    em.dma("wp", lambda g: g.dma_start(out=V["onesr"][:], in_=V["d_or"]))
    pre_oc = em.dma("wp", lambda g: g.dma_start(out=V["onesc"][:],
                                                in_=V["d_oc"]))
    for eng in ("pe", "dve", "act"):
        em.wait(eng, "wp", pre_oc)
        em.wait(eng, "rl", pre_if)
    em.op("dve", lambda v: v.memset(V["epsc"][:], LN_EPS))
    eps_ready = em.mark("dve")
    em.wait("act", "dve", eps_ready)

    def stats_unit(m, hh, slot, first):
        pass

    def emit_stats_mms(m, hh, slot, first):
        em.op("pe", lambda t: t.matmul(
            PB[2 + hh][0:1, :], V["onesc"][:], sbf_slot(slot),
            start=first, stop=(m == M - 1), skip_group_check=True))
        em.op("pe", lambda t: t.matmul(
            PB[4 + hh][0:1, :], V["onesc"][:], mbf_slot(slot),
            start=first, stop=(m == M - 1), skip_group_check=True))
        em.mark("pe")
        slot_set(("sbf", slot), "pe")
        slot_set(("mbf", slot), "pe")

    def ln_stats_pass(src):
        for b in (2, 3, 4, 5):
            bank_wait(b, "pe")
        first = True
        units = [(m, hh) for m in range(M) for hh in (0, 1)]
        pend = {}

        def issue(u):
            m, hh = units[u]
            slot = u % 3
            slot_wait(("stage", slot), "sp")
            pend[u] = (em.dma("sd%d" % slot, lambda s: s.dma_start(
                out=stg_slot(slot),
                in_=src[:, m * NT + hh * HH: m * NT + hh * HH + HH])), slot)

        issue(0)
        issue(1)
        for u, (m, hh) in enumerate(units):
            if True:
                if u + 2 < len(units):
                    issue(u + 2)
                dv, slot = pend.pop(u)
                em.wait("dve", "sd%d" % slot, dv)
                slot_wait(("sbf", slot % 2), "dve")
                em.op("dve", lambda v: v.tensor_copy(sbf_slot(slot % 2),
                                                     stg_slot(slot)))
                cm = em.mark("dve")
                slot_set(("stage", slot), "dve")
                em.wait("act", "dve", cm)
                slot_wait(("mbf", slot % 2), "act")
                em.op("act", lambda a: a.activation(
                    mbf_slot(slot % 2), sbf_slot(slot % 2), AF.Square))
                sq = em.mark("act")
                em.wait("pe", "act", sq)
                emit_stats_mms(m, hh, slot % 2, m == 0)
        S["ln_stats_done"] = em.c["pe"]

    def ln_finalize_apply(src, src_ready):
        t_mean = sc[0:1, :]
        t_var = stage[0:1, 0:512]
        t_tmp = stage[0:1, 512:1024]
        for hh in (0, 1):
            em.wait("dve", "pe", S["ln_stats_done"])
            em.waitp("dve", sc_free[0])
            slot_wait(("stage", 0), "dve")
            slot_wait(("stage", 1), "dve")
            em.op("dve", lambda v: v.tensor_scalar(
                t_mean, PB[2 + hh][0:1, :], 1.0 / 4096.0, None, ALU.mult))
            em.fence("dve")
            em.op("dve", lambda v: v.tensor_scalar(
                t_var, PB[4 + hh][0:1, :], 1.0 / 4096.0, None, ALU.mult))
            em.fence("dve")
            em.op("dve", lambda v: v.tensor_mul(t_tmp, t_mean, t_mean))
            em.fence("dve")
            em.op("dve", lambda v: v.tensor_sub(t_var, t_var, t_tmp))
            em.fence("dve")
            em.op("dve", lambda v: v.tensor_copy(V["m16"][:], t_mean))
            f1 = em.mark("dve")
            em.wait("act", "dve", f1)
            em.op("act", lambda a: a.activation(t_tmp, t_var, AF.Sqrt,
                                                bias=V["epsc"][0:1, :]))
            f2 = em.mark("act")
            em.wait("dve", "act", f2)
            em.op("dve", lambda v: v.reciprocal(t_var, t_tmp))
            em.fence("dve")
            em.op("dve", lambda v: v.tensor_copy(V["r16"][:], t_var))
            f3 = em.mark("dve")
            sc_free[0] = ("dve", em.c["dve"])
            slot_set(("stage", 0), "dve")
            slot_set(("stage", 1), "dve")
            bank_free(2 + hh, "dve")
            bank_free(4 + hh, "dve")
            em.wait("pe", "dve", f3)
            bank_wait(0, "pe")
            bank_wait(1, "pe")
            em.op("pe", lambda t: t.matmul(PB[0][:], V["onesr"][:],
                                           V["m16"][:], start=True, stop=True))
            em.op("pe", lambda t: t.matmul(PB[1][:], V["onesr"][:],
                                           V["r16"][:], start=True, stop=True))
            bc = em.mark("pe")
            em.wait("dve", "pe", bc)
            slot_wait(("sbf", 0), "dve")
            slot_wait(("sbf", 1), "dve")
            em.op("dve", lambda v: v.tensor_copy(stagebf[:, 0:512], PB[0][:]))
            em.op("dve", lambda v: v.tensor_copy(stagebf[:, 512:1024],
                                                 PB[1][:]))
            mrev = em.mark("dve")
            em.wait("pool", "dve", mrev)
            apend = {}

            def aissue(m, hh=hh):
                slot = m % 3
                slot_wait(("stage", slot), "sp")
                apend[m] = (em.dma("sd%d" % slot, lambda s: s.dma_start(
                    out=stg_slot(slot),
                    in_=src[:, m * NT + hh * HH: m * NT + hh * HH + HH])),
                    slot)

            if src_ready:
                for q in ("sd0", "sd1", "sd2", "sd3"):
                    em.wait("sp", q, em.c[q])
                src_ready = False
            aissue(0)
            aissue(1)
            for m in range(M):
                if m + 2 < M:
                    aissue(m + 2)
                dv, slot = apend.pop(m)
                eng = "pool" if m % 3 == 2 else "dve"
                em.wait(eng, "sd%d" % slot, dv)
                if eng == "dve":
                    em.wait("dve", "pe", bc)
                    em.op("dve", lambda v: v.tensor_sub(
                        stg_slot(slot), stg_slot(slot), PB[0][:]))
                    em.op("dve", lambda v: v.tensor_mul(
                        xln(m, hh), stg_slot(slot), PB[1][:]))
                else:
                    em.op("pool", lambda g: g.tensor_sub(
                        stg_slot(slot), stg_slot(slot), stagebf[:, 0:512]))
                    em.op("pool", lambda g: g.tensor_mul(
                        xln(m, hh), stg_slot(slot), stagebf[:, 512:1024]))
                em.mark(eng)
                slot_set(("stage", slot), eng)
            bank_free(0, "dve")
            bank_free(1, "dve")
            slot_set(("sbf", 0), "pool")
            slot_set(("sbf", 1), "pool")
        S["xln_ready"] = [("dve", em.c["dve"]), ("pool", em.c["pool"])]

    def gating():
        for pr in S["xln_ready"]:
            em.waitp("pe", pr)
        pass
        bank_wait(2, "pe")
        bank_wait(3, "pe")
        for hh in (0, 1):
            for k in range(KK):
                em.op("pe", lambda t: t.matmul(
                    PB[2 + hh][0:16, :], V["gatewT"][:, 16 * k:16 * k + 16],
                    xln(k, hh), start=(k == 0), stop=(k == KK - 1)))
        lg = em.mark("pe")
        em.wait("dve", "pe", lg)
        slot_wait(("stage", 0), "dve")
        slot_wait(("stage", 1), "dve")
        em.op("dve", lambda v: v.tensor_copy(stage[0:16, 0:512],
                                             PB[2][0:16, :]))
        em.op("dve", lambda v: v.tensor_copy(stage[0:16, 512:1024],
                                             PB[3][0:16, :]))
        lge = em.mark("dve")
        bank_free(2, "dve")
        bank_free(3, "dve")
        em.wait("pe", "dve", lge)
        bank_wait(4, "pe")
        for b in range(8):
            hh, bb = divmod(b, 4)
            em.op("pe", lambda t: t.transpose(
                PB[4][:, 16 * b:16 * b + 16],
                stage[0:16, hh * 512 + 128 * bb: hh * 512 + 128 * bb + 128],
                V["identF"][:]))
        ltr = em.mark("pe")
        em.wait("dve", "pe", ltr)
        em.op("dve", lambda v: v.tensor_copy(V["ltok"][:], PB[4][:, 0:128]))
        em.mark("dve")
        bank_free(4, "dve")
        slot_set(("stage", 0), "dve")
        slot_set(("stage", 1), "dve")
        em.waitp("dve", wtok_free[0])
        for b in range(8):
            lt = V["ltok"][:, 16 * b:16 * b + 16]
            c_m1, c_m2 = gsc[:, 0:1], gsc[:, 1:2]
            c_e2, c_den, c_r = gsc[:, 2:3], gsc[:, 3:4], gsc[:, 4:5]
            r_t, r_u, r_x = gsc[:, 16:32], gsc[:, 32:48], gsc[:, 48:64]
            em.op("dve", lambda v: v.reduce_max(c_m1, lt, axis=AX.X))
            em.fence("dve")
            em.op("dve", lambda v: v.tensor_scalar(r_t, lt, c_m1, None,
                                                   ALU.subtract))
            em.fence("dve")
            em.op("dve", lambda v: v.tensor_scalar(r_u, r_t, 0.0, None,
                                                   ALU.is_ge))
            em.fence("dve")
            em.op("dve", lambda v: v.scalar_tensor_tensor(
                r_x, r_u, -1e30, r_t, ALU.mult, ALU.add))
            em.fence("dve")
            em.op("dve", lambda v: v.reduce_max(c_m2, r_x, axis=AX.X))
            g1 = em.mark("dve")
            em.wait("act", "dve", g1)
            em.op("act", lambda a: a.activation(c_e2, c_m2, AF.Exp))
            em.fence("act")
            em.op("act", lambda a: a.activation(r_x, r_t, AF.Exp))
            g2 = em.mark("act")
            em.wait("dve", "act", g2)
            em.op("dve", lambda v: v.tensor_scalar(c_den, c_e2, 1.0, None,
                                                   ALU.add))
            em.fence("dve")
            em.op("dve", lambda v: v.reciprocal(c_r, c_den))
            em.fence("dve")
            em.op("dve", lambda v: v.tensor_scalar(r_u, r_t, c_m2, None,
                                                   ALU.is_ge))
            em.fence("dve")
            em.op("dve", lambda v: v.tensor_mul(r_t, r_u, r_x))
            em.fence("dve")
            em.op("dve", lambda v: v.tensor_scalar(
                V["wtok"][:, 16 * b:16 * b + 16], r_t, c_r, 0.5,
                ALU.mult, ALU.mult))
            em.mark("dve")
        S["wtok_ready"] = em.c["dve"]

    fin_at = {}
    for e in range(NE):
        fin_at.setdefault((448 * e + 447) // 128, []).append(e)

    def expert_spans(e):
        lo, hi = 448 * e, 448 * (e + 1)
        spans = []
        c0 = lo // 128
        if lo % 128:
            spans.append((slice(64, 128), c0, 1))
            c0 += 1
        nfull = hi // 128 - c0
        if nfull:
            spans.append((slice(0, 128), c0, nfull))
        if hi % 128:
            spans.append((slice(0, 64), hi // 128, 1))
        return spans

    def hspan_ap(psl, c0, hh, nch):
        base = regB[psl, c0 * NT + hh * HH: c0 * NT + hh * HH + HH]
        return repl(base, [[NT, nch], [1, HH]])

    def wspan_ap(psl, nch, hh2):
        return repl(PB[2 + hh2][psl, :], [[0, nch], [1, HH]])

    def ffn_gu(wg_src, wu_src):
        hmul = {}
        wpend = {}
        for c in range(C):
            sg0 = wslab_push(wg_src[c, :, 0:2048])
            sg1 = wslab_push(wg_src[c, :, 2048:4096])
            su0 = wslab_push(wu_src[c, :, 0:2048])
            gb0, gb1 = PB[0], PB[1]
            for e in fin_at.get(c, []):
                em.wait("pe", "dve", S["wtok_ready"])
                em.waitp("pe", bfg[0])
                for b in range(8):
                    em.op("pe", lambda t: t.transpose(
                        PBT[0:1, 128 * b:128 * b + 128],
                        V["wtok"][:, 16 * b + e:16 * b + e + 1],
                        V["identB"][:]))
                wtr = em.mark("pe")
                em.wait("dve", "pe", wtr)
                em.op("dve", lambda v: v.tensor_copy(V["wrow16"][0:1, 0:512],
                                                     PBT[0:1, 0:512]))
                em.op("dve", lambda v: v.tensor_copy(
                    V["wrow16"][0:1, 512:1024], PBT[0:1, 512:1024]))
                wre = em.mark("dve")
                bfg[0] = ("dve", em.c["dve"])
                em.wait("pe", "dve", wre)
                wbs = []
                for hh2 in (0, 1):
                    bank_wait(2 + hh2, "pe")
                    em.op("pe", lambda t: t.matmul(
                        PB[2 + hh2][:], V["onesr"][:],
                        V["wrow16"][0:1, 512 * hh2:512 * hh2 + 512],
                        start=True, stop=True))
                    wbs.append(em.mark("pe"))
                wpend[e] = wbs
            em.wait("pe", "dve", S["xln_ready"]) if isinstance(
                S["xln_ready"], int) else [em.waitp("pe", pr)
                                           for pr in S["xln_ready"]]
            bank_wait(0, "pe")
            bank_wait(1, "pe")
            for k in range(KK):
                slab = sg0 if k < 16 else sg1
                if k in (0, 16):
                    wslab_wait(slab)
                em.op("pe", lambda t: t.matmul(
                    gb0[:], wtile(slab, k % 16), xln(k, 0),
                    start=(k == 0), stop=(k == KK - 1)))
                em.op("pe", lambda t: t.matmul(
                    gb1[:], wtile(slab, k % 16), xln(k, 1),
                    start=(k == 0), stop=(k == KK - 1)))
                if k == 15:
                    wslab_done(sg0)
            wslab_done(sg1)
            su1 = wslab_push(wu_src[c, :, 2048:4096])
            gdone = em.c["pe"]
            bank_wait(4, "pe")
            bank_wait(5, "pe")
            for k in range(KK):
                slab = su0 if k < 16 else su1
                if k in (0, 16):
                    wslab_wait(slab)
                em.op("pe", lambda t: t.matmul(
                    PB[4][:], wtile(slab, k % 16), xln(k, 0),
                    start=(k == 0), stop=(k == KK - 1)))
                em.op("pe", lambda t: t.matmul(
                    PB[5][:], wtile(slab, k % 16), xln(k, 1),
                    start=(k == 0), stop=(k == KK - 1)))
                if k == 15:
                    wslab_done(su0)
            wslab_done(su1)
            udone = em.c["pe"]
            for hh in (0, 1):
                gb = gb0 if hh == 0 else gb1
                ub = PB[4 + hh]
                em.wait("act", "pe", gdone)
                slot_wait(("mbf", hh), "act")
                em.op("act", lambda a: a.activation(mbf_slot(hh), gb[:],
                                                    AF.Silu))
                sm = em.mark("act")
                bank_free(hh, "act")
                em.wait("dve", "act", sm)
                em.wait("dve", "pe", udone)
                em.op("dve", lambda v: v.tensor_mul(hv(c, hh), mbf_slot(hh),
                                                    ub[:]))
                hmul[(c, hh)] = em.mark("dve")
                slot_set(("mbf", hh), "dve")
                bank_free(4 + hh, "dve")
            for e in fin_at.get(c, []):
                wbs = wpend.pop(e)
                for hh2 in (0, 1):
                    em.wait("dve", "pe", wbs[hh2])
                    for (psl, c0, nch) in expert_spans(e):
                        em.op("dve", lambda v: v.tensor_mul(
                            hspan_ap(psl, c0, hh2, nch),
                            hspan_ap(psl, c0, hh2, nch),
                            wspan_ap(psl, nch, hh2)))
                    em.mark("dve")
                    bank_free(2 + hh2, "dve")
        wtok_free[0] = ("pe", em.c["pe"])
        S["h_ready"] = em.c["dve"]

    def ffn_down(wd_src, x_src, dst_dram, stats):
        if stats:
            for b in (2, 3, 4, 5):
                bank_wait(b, "pe")
        wv = 0
        dpend = {}

        def dissue(u):
            m2, hh2 = divmod(u, 2)
            slot = u % 3
            slot_wait(("stage", slot), "sp")
            dpend[u] = (em.dma("sd%d" % slot, lambda s: s.dma_start(
                out=stg_slot(slot),
                in_=x_src[:, m2 * NT + hh2 * HH: m2 * NT + hh2 * HH + HH])),
                slot)

        for m in range(M):
            if stats:
                i0, i1 = 0, 1
            else:
                i0, i1 = 2 * (m % 2), 2 * (m % 2) + 1
            b0, b1 = PB[i0], PB[i1]
            sl = [wslab_push(wd_src[m, :, 2048 * q: 2048 * q + 2048])
                  for q in range(3)]
            em.wait("pe", "dve", S["h_ready"])
            bank_wait(i0, "pe")
            bank_wait(i1, "pe")
            for c in range(C):
                q, t = divmod(c, 16)
                if c == 16:
                    sl.append(wslab_push(wd_src[m, :, 6144:7168]))
                if t == 0:
                    wslab_wait(sl[q])
                em.op("pe", lambda te: te.matmul(
                    b0[:], wtile(sl[q], t), hv(c, 0),
                    start=(c == 0), stop=(c == C - 1)))
                em.op("pe", lambda te: te.matmul(
                    b1[:], wtile(sl[q], t), hv(c, 1),
                    start=(c == 0), stop=(c == C - 1)))
                if t == 15 or c == C - 1:
                    wslab_done(sl[q])
            ddone = em.c["pe"]
            for hh in (0, 1):
                bank = b0 if hh == 0 else b1
                u = 2 * m + hh
                if u == 0:
                    dissue(0)
                    dissue(1)
                if u + 2 < 2 * M:
                    dissue(u + 2)
                dv, slot = dpend.pop(u)
                em.wait("dve", "sd%d" % slot, dv)
                em.wait("dve", "pe", ddone)
                em.op("dve", lambda v: v.tensor_add(stg_slot(slot),
                                                    stg_slot(slot), bank[:]))
                if stats:
                    slot_wait(("sbf", slot % 2), "dve")
                    em.op("dve", lambda v: v.tensor_copy(sbf_slot(slot % 2),
                                                         stg_slot(slot)))
                ev = em.mark("dve")
                bank_free(i0 + hh, "dve")
                em.wait("sp", "dve", ev)
                wv = em.dma("sd%d" % slot, lambda s: s.dma_start(
                    out=dst_dram[:, m * NT + hh * HH: m * NT + hh * HH + HH],
                    in_=stg_slot(slot)))
                slot_set(("stage", slot), "sd%d" % slot)
                if stats:
                    em.wait("act", "dve", ev)
                    slot_wait(("mbf", slot % 2), "act")
                    em.op("act", lambda a: a.activation(
                        mbf_slot(slot % 2), sbf_slot(slot % 2), AF.Square))
                    sq = em.mark("act")
                    em.wait("pe", "act", sq)
                    emit_stats_mms(m, hh, slot % 2, m == 0)
        if stats:
            S["ln_stats_done"] = em.c["pe"]
        return wv

    def proj(w_src, kind, rhs_fn, bias_idx, x_src=None, dst_dram=None,
             park=None, stats=False):
        evmarks, parkw = [], []
        wrlast = 0
        opend = {}

        def oissue(u):
            m2, hh2 = divmod(u, 2)
            slot = u % 3
            slot_wait(("stage", slot), "sp")
            opend[u] = (em.dma("sd%d" % slot, lambda s: s.dma_start(
                out=stg_slot(slot),
                in_=x_src[:, m2 * NT + hh2 * HH: m2 * NT + hh2 * HH + HH])),
                slot)

        if stats:
            for b in (2, 3, 4, 5):
                bank_wait(b, "pe")
        for m in range(M):
            s0 = wslab_push(w_src[m, :, 0:2048])
            s1 = wslab_push(w_src[m, :, 2048:4096])
            if stats:
                i0, i1 = 0, 1
            else:
                i0, i1 = 2 * (m % 2), 2 * (m % 2) + 1
            b0, b1 = PB[i0], PB[i1]
            for pr in (S["rhs_ready"] if isinstance(S["rhs_ready"], list)
                       else [S["rhs_ready"]]):
                em.waitp("pe", pr)
            bank_wait(i0, "pe")
            bank_wait(i1, "pe")
            for k in range(KK):
                slab = s0 if k < 16 else s1
                if k in (0, 16):
                    wslab_wait(slab)
                em.op("pe", lambda t: t.matmul(
                    b0[:], wtile(slab, k % 16), rhs_fn(k, 0),
                    start=(k == 0), stop=(k == KK - 1)))
                em.op("pe", lambda t: t.matmul(
                    b1[:], wtile(slab, k % 16), rhs_fn(k, 1),
                    start=(k == 0), stop=(k == KK - 1)))
                if k == 15:
                    wslab_done(s0)
            wslab_done(s1)
            pdone = em.c["pe"]
            bias = V["biases"][:, 32 * bias_idx + m: 32 * bias_idx + m + 1]
            if kind in ("q", "k"):
                base = 0 if kind == "q" else 16384
                em.wait("act", "pe", pdone)
                em.op("act", lambda a: a.activation(
                    regB[:, base + m * HH: base + m * HH + HH], b0[:],
                    AF.Identity, bias=bias))
                e0 = em.mark("act")
                bank_free(i0, "act")
                slot = m % 2
                slot_wait(("mbf", slot), "act")
                em.op("act", lambda a: a.activation(mbf_slot(slot), b1[:],
                                                    AF.Identity, bias=bias))
                e1 = em.mark("act")
                bank_free(i1, "act")
                em.wait("sp", "act", e1)
                pw = em.dma("pk%d" % slot, lambda s: s.dma_start(
                    out=park[:, m * HH: m * HH + HH], in_=mbf_slot(slot)))
                parkw.append(pw)
                slot_set(("mbf", slot), "pk%d" % slot)
                evmarks.append(e1)
            elif kind == "v":
                em.wait("act", "pe", pdone)
                if m == 0:
                    em.wait("act", "pe", S["scores_h1_done"])
                for hh in (0, 1):
                    bank = b0 if hh == 0 else b1
                    base = regB[:, 16384 + hh * 16384 + m:
                                16384 + hh * 16384 + m + 1]
                    dst = repl(base, [[32, HH]])
                    em.op("act", lambda a: a.activation(dst, bank[:], AF.Identity,
                                                        bias=bias))
                    evmarks.append(em.mark("act"))
                    bank_free(i0 + hh, "act")
            elif kind == "o":
                for hh in (0, 1):
                    bank = b0 if hh == 0 else b1
                    slot = (2 * m + hh) % 2
                    em.wait("act", "pe", pdone)
                    em.waitp("act", sc_free[0])
                    em.op("act", lambda a: a.activation(sc[:, :], bank[:],
                                                        AF.Identity, bias=bias))
                    ab = em.mark("act")
                    bank_free(i0 + hh, "act")
                    u = 2 * m + hh
                    if u == 0:
                        oissue(0)
                        oissue(1)
                    if u + 2 < 2 * M:
                        oissue(u + 2)
                    dv, slot = opend.pop(u)
                    em.wait("dve", "sd%d" % slot, dv)
                    em.wait("dve", "act", ab)
                    em.op("dve", lambda v: v.tensor_add(stg_slot(slot),
                                                        stg_slot(slot),
                                                        sc[:, :]))
                    if stats:
                        slot_wait(("sbf", slot % 2), "dve")
                        em.op("dve", lambda v: v.tensor_copy(
                            sbf_slot(slot % 2), stg_slot(slot)))
                    ev = em.mark("dve")
                    sc_free[0] = ("dve", em.c["dve"])
                    em.wait("sp", "dve", ev)
                    wrlast = em.dma("sd%d" % slot, lambda s: s.dma_start(
                        out=dst_dram[:, m * NT + hh * HH:
                                     m * NT + hh * HH + HH],
                        in_=stg_slot(slot)))
                    slot_set(("stage", slot), "sd%d" % slot)
                    if stats:
                        em.wait("act", "dve", ev)
                        slot_wait(("mbf", slot % 2), "act")
                        em.op("act", lambda a: a.activation(
                            mbf_slot(slot % 2), sbf_slot(slot % 2), AF.Square))
                        sq = em.mark("act")
                        em.wait("pe", "act", sq)
                        emit_stats_mms(m, hh, slot % 2, m == 0)
        if stats:
            S["ln_stats_done"] = em.c["pe"]
        return evmarks, parkw, wrlast

    def scores_half(half, qk_ready):
        for kb in range(8):
            t0 = 64 * kb
            bank = PB[kb % 2]
            bank_wait(kb % 2, "pe")
            em.waitp("pe", qk_ready)
            for s in range(64):
                a = s % 4
                b = s // 4
                tl = t0 + s
                qs = regB[:, tl:16384:HH]
                ks = regB[:, 16384 + tl:32768:HH]
                em.op("pe", lambda t: t.matmul(
                    bank[32 * a:32 * a + 32, 32 * b:32 * b + 32], qs, ks,
                    start=True, stop=True,
                    tile_position=(0, 32 * a), skip_group_check=True))
            mm = em.mark("pe")
            em.wait("dve", "pe", mm)
            em.waitp("dve", sc_free[0])
            em.op("dve", lambda v: v.tensor_scalar(
                sc[:, :], bank[:], THR / INV_SQRT_D, 100.0, ALU.is_gt,
                ALU.mult))
            em.fence("dve")
            em.op("dve", lambda v: v.scalar_tensor_tensor(
                sc[:, :], bank[:], INV_SQRT_D, sc[:, :], ALU.mult, ALU.add))
            em.fence("dve")
            em.op("dve", lambda v: v.reduce_max(gsc[:, 0:16],
                                                seg32(sc[:, :]), axis=AX.X))
            em.fence("dve")
            em.op("dve", lambda v: v.tensor_sub(
                seg32(sc[:, :]), seg32(sc[:, :]), bcast0(gsc[:, 0:16], 32)))
            d1 = em.mark("dve")
            bank_free(kb % 2, "dve")
            em.wait("act", "dve", d1)
            slot_wait(("sbf", 0), "act")
            em.op("act", lambda a: a.activation(stagebf[:, 0:512], sc[:, :],
                                                AF.Exp))
            ex = em.mark("act")
            sc_free[0] = ("act", em.c["act"])
            em.wait("dve", "act", ex)
            slot_wait(("sbf", 1), "dve")
            em.op("dve", lambda v: v.reduce_sum(
                gsc[:, 16:32], seg32(stagebf[:, 0:512]), axis=AX.X))
            em.fence("dve")
            em.op("dve", lambda v: v.reciprocal(gsc[:, 32:48], gsc[:, 16:32]))
            em.fence("dve")
            em.op("dve", lambda v: v.tensor_mul(
                seg32(stagebf[:, 512:1024]), seg32(stagebf[:, 0:512]),
                bcast0(gsc[:, 32:48], 32)))
            em.fence("dve")
            atcol = 49152 + (8 * half + kb) * 512
            for j in range(16):
                em.op("dve", lambda v: v.transpose(
                    regB[:, atcol + 32 * j: atcol + 32 * j + 32],
                    stagebf[:, 512 + 32 * j: 512 + 32 * j + 32]))
            em.mark("dve")
            slot_set(("sbf", 0), "dve")
            slot_set(("sbf", 1), "dve")
        S["scores_done_%d" % half] = em.c["pe"]
        S["attnT_done_%d" % half] = em.c["dve"]

    def av_phase():
        em.wait("pe", "act", S["v_evac_done"])
        em.wait("pe", "dve", S["attnT_done_1"])
        for grp in range(64):
            t0 = 16 * grp
            avb = 4 + grp % 2
            avmm = 0
            for i in range(4):
                bk = i % 2
                em.waitp("pe", bfg[bk])
                em.op("pe", lambda t: t.transpose(
                    bfbank[bk][:, 0:128],
                    regB[:, 16384 + (t0 + 4 * i) * 32:
                         16384 + (t0 + 4 * i) * 32 + 128],
                    V["identB"][:]))
                vtr = em.mark("pe")
                em.wait("act", "pe", vtr)
                vslot = i % 2
                em.waitp("act", vp_free[vslot])
                em.op("act", lambda a: a.activation(
                    V["vpack"][:, 128 * vslot:128 * vslot + 128],
                    bfbank[bk][:, 0:128], AF.Identity))
                vpe = em.mark("act")
                bfg[bk] = ("act", em.c["act"])
                em.wait("pe", "act", vpe)
                if i == 0:
                    bank_wait(avb, "pe")
                for j in range(4):
                    tg = t0 + 4 * i + j
                    kbg = tg // 64
                    s64 = tg % 64
                    acol = 49152 + kbg * 512 + 32 * (s64 // 4)
                    em.op("pe", lambda t: t.matmul(
                        PB[avb][32 * j:32 * j + 32, 128 * i:128 * i + 128],
                        regB[32 * j:32 * j + 32, acol:acol + 32],
                        V["vpack"][32 * j:32 * j + 32,
                                   128 * vslot:128 * vslot + 128],
                        start=True, stop=True,
                        tile_position=(32 * j, 32 * j),
                        skip_group_check=True))
                avmm = em.mark("pe")
                vp_free[vslot] = ("pe", em.c["pe"])
            em.wait("dve", "pe", avmm)
            slot_wait(("mbf", 0), "dve")
            em.op("dve", lambda v: v.tensor_copy(miscbf[:, 0:512],
                                                 PB[avb][:]))
            ae = em.mark("dve")
            bank_free(avb, "dve")
            for i in range(4):
                bk = i % 2
                em.waitp("pe", bfg[bk])
                em.wait("pe", "dve", ae)
                em.op("pe", lambda t: t.transpose(
                    bfbank[bk][:, 128:256],
                    miscbf[:, 128 * i:128 * i + 128], V["identB"][:]))
                otr = em.mark("pe")
                em.wait("dve", "pe", otr)
                srcap = repl(bfbank[bk][:, 128:129], [[32, 4], [1, 32]])
                dst = repl(regA[:, t0 + 4 * i: t0 + 4 * i + 1],
                           [[1, 4], [NT, 32]])
                em.op("dve", lambda v: v.tensor_copy(dst, srcap))
                em.mark("dve")
                bfg[bk] = ("dve", em.c["dve"])
            slot_set(("mbf", 0), "dve")
        S["ao_ready"] = em.c["dve"]

    def fin():
        if DBG_X:
            em.wait("sp", "pe", em.c["pe"])
            em.wait("sp", "dve", em.c["dve"])
            em.wait("sp", "act", em.c["act"])
            em.dma("rl", lambda s: s.dma_start(out=V["d_dbgA"], in_=regA[:, :]))
            em.dma("rl", lambda s: s.dma_start(out=V["d_dbgB"], in_=regB[:, :]))
            em.dma("rl", lambda s: s.dma_start(out=V["d_dbgW"],
                                               in_=V["wtok"][:, :]))
            em.dma("rl", lambda s: s.dma_start(out=V["d_dbgL"],
                                               in_=V["ltok"][:, :]))
        for q in Em.QENG:
            em.wait("sp", q, em.c[q])

    ln_stats_pass(V["d_x"])
    if PHLIM <= 0:
        return fin()
    ln_finalize_apply(V["d_x"], False)
    if PHLIM <= 1:
        return fin()
    gating()
    if PHLIM <= 2:
        return fin()
    ffn_gu(V["d_g"], V["d_u"])
    if PHLIM <= 3:
        return fin()
    x1w = ffn_down(V["d_d"], V["d_x"], V["d_x1"], stats=True)
    if PHLIM <= 4:
        return fin()
    ln_finalize_apply(V["d_x1"], True)
    S["rhs_ready"] = S["xln_ready"]
    if PHLIM <= 5:
        return fin()
    qe, qpw, _ = proj(V["d_wq"], "q", xln, 0, park=V["d_qp"])
    ke, kpw, _ = proj(V["d_wk"], "k", xln, 1, park=V["d_kp"])
    if PHLIM <= 6:
        return fin()
    scores_half(0, ("act", ke[-1]))
    if PHLIM <= 7:
        return fin()
    em.wait("sp", "pk0", em.c["pk0"])
    em.wait("sp", "pk1", em.c["pk1"])
    em.wait("sp", "pe", S["scores_done_0"])
    em.dma("rl", lambda s: s.dma_start(out=regB[:, 0:16384],
                                       in_=V["d_qp"][:, :]))
    rl2 = em.dma("rl", lambda s: s.dma_start(out=regB[:, 16384:32768],
                                             in_=V["d_kp"][:, :]))
    scores_half(1, ("rl", rl2))
    S["scores_h1_done"] = S["scores_done_1"]
    if PHLIM <= 8:
        return fin()
    ve, _, _ = proj(V["d_wv"], "v", xln, 2)
    S["v_evac_done"] = ve[-1]
    if PHLIM <= 9:
        return fin()
    av_phase()
    S["rhs_ready"] = [("dve", S["ao_ready"])]
    if PHLIM <= 10:
        return fin()

    def ao_rhs(k, hh):
        return regA[:, k * NT + hh * HH: k * NT + hh * HH + HH]

    _, _, x2w = proj(V["d_wo"], "o", ao_rhs, 3, x_src=V["d_x1"],
                     dst_dram=V["d_x2"], stats=True)
    if PHLIM <= 11:
        return fin()
    ln_finalize_apply(V["d_x2"], True)
    if PHLIM <= 12:
        return fin()
    gating()
    ffn_gu(V["d_g"], V["d_u"])
    if PHLIM <= 13:
        return fin()
    ffn_down(V["d_d"], V["d_x2"], V["d_y"], stats=False)
    fin()


def build_nc():
    nc = bass.Bass("TRN2", target_bir_lowering=False,
                   detect_race_conditions=False)
    V = {}
    V["d_x"] = nc.dram_tensor("xin", [128, M * NT], F32,
                              kind="ExternalInput").ap()
    V["d_y"] = nc.dram_tensor("y", [128, M * NT], F32,
                              kind="ExternalOutput").ap()
    V["d_g"] = nc.dram_tensor("wg", [C, 128, 4096], F32,
                              kind="ExternalInput").ap()
    V["d_u"] = nc.dram_tensor("wu", [C, 128, 4096], F32,
                              kind="ExternalInput").ap()
    V["d_d"] = nc.dram_tensor("wd", [M, 128, 7168], F32,
                              kind="ExternalInput").ap()
    for nm in ("wq", "wk", "wv", "wo"):
        V["d_" + nm] = nc.dram_tensor(nm, [M, 128, 4096], F32,
                                      kind="ExternalInput").ap()
    V["d_gw"] = nc.dram_tensor("gw", [128, 512], F32,
                               kind="ExternalInput").ap()
    V["d_b"] = nc.dram_tensor("bs", [128, 128], F32,
                              kind="ExternalInput").ap()
    V["d_id"] = nc.dram_tensor("ident_d", [128, 128], F32,
                               kind="ExternalInput").ap()
    V["d_or"] = nc.dram_tensor("onesr_d", [1, 128], F32,
                               kind="ExternalInput").ap()
    V["d_oc"] = nc.dram_tensor("onesc_d", [128, 1], F32,
                               kind="ExternalInput").ap()
    if DBG_X:
        V["d_dbgA"] = nc.dram_tensor("dbgA", [128, 32768], BF16,
                                     kind="ExternalOutput").ap()
        V["d_dbgB"] = nc.dram_tensor("dbgB", [128, 57344], BF16,
                                     kind="ExternalOutput").ap()
        V["d_dbgW"] = nc.dram_tensor("dbgW", [128, 128], BF16,
                                     kind="ExternalOutput").ap()
        V["d_dbgL"] = nc.dram_tensor("dbgL", [128, 128], F32,
                                     kind="ExternalOutput").ap()
    xkind = "ExternalOutput" if DBG_X else "Internal"
    V["d_x1"] = nc.dram_tensor("x1d", [128, M * NT], F32, kind=xkind).ap()
    V["d_x2"] = nc.dram_tensor("x2d", [128, M * NT], F32, kind=xkind).ap()
    V["d_qp"] = nc.dram_tensor("qpark", [128, 16384], BF16,
                               kind="Internal").ap()
    V["d_kp"] = nc.dram_tensor("kpark", [128, 16384], BF16,
                               kind="Internal").ap()

    guards = []

    def sb(name, shape, dtype):
        g = nc.sbuf_tensor(name, shape, dtype)
        t = g.__enter__()
        guards.append(g)
        return t

    def ps(name, shape, dtype):
        g = nc.psum_tensor(name, shape, dtype)
        t = g.__enter__()
        guards.append(g)
        return t

    V["regA"] = sb("regA", [128, 32768], BF16)
    V["regB"] = sb("regB", [128, 57344], BF16)
    V["wring"] = sb("wring", [128, 6144], BF16)
    V["stage"] = sb("stage", [128, 1536], F32)
    V["sc"] = sb("sc", [128, 512], F32)
    V["stagebf"] = sb("stagebf", [128, 1024], BF16)
    V["miscbf"] = sb("miscbf", [128, 1024], BF16)
    V["vpack"] = sb("vpack", [128, 256], BF16)
    V["m16"] = sb("m16", [1, 512], BF16)
    V["r16"] = sb("r16", [1, 512], BF16)
    V["wrow16"] = sb("wrow16", [1, 1024], BF16)
    V["gsc"] = sb("gsc", [128, 64], F32)
    V["ltok"] = sb("ltok", [128, 128], F32)
    V["wtok"] = sb("wtok", [128, 128], BF16)
    V["gatewT"] = sb("gatewT", [128, 512], BF16)
    V["biases"] = sb("biases", [128, 128], F32)
    V["identB"] = sb("identB", [128, 128], BF16)
    V["identF"] = sb("identF", [16, 16], F32)
    V["onesr"] = sb("onesr", [1, 128], BF16)
    V["epsc"] = sb("epsc", [128, 1], F32)
    V["onesc"] = sb("onesc", [128, 1], BF16)

    V["pb"] = [ps("pb%d" % i, [128, 512], F32) for i in range(6)]
    V["pbt"] = ps("pbt", [128, 1024], BF16)
    V["pbw"] = ps("pbw", [128, 1024], BF16)

    sems = {}
    for s in ("pe", "dve", "act", "pool", "sp", "wd0", "wd1", "wd2", "wp",
              "sd0", "sd1", "sd2", "sd3", "pk0", "pk1", "rl"):
        g = nc.semaphore(s + "_sem")
        sems[s] = g.__enter__()
        guards.append(g)

    blk = nc.Block()
    block = blk.__enter__()
    for mode, deco in (("pool", block.gpsimd), ("pe", block.tensor),
                       ("dve", block.vector), ("act", block.scalar),
                       ("sp", block.sync)):
        def body(eng, mode=mode):
            build_schedule(Em(nc, mode, sems), V)
        deco(body)
    blk.__exit__(None, None, None)
    for g in reversed(guards):
        g.__exit__(None, None, None)
    return nc


def _prep_weights(inp):
    g = np.ascontiguousarray(
        inp["L0_ffn_g"].reshape(C, 128, KK, 128).transpose(0, 3, 2, 1)
    ).reshape(C, 128, 4096)
    u = np.ascontiguousarray(
        inp["L0_ffn_u"].reshape(C, 128, KK, 128).transpose(0, 3, 2, 1)
    ).reshape(C, 128, 4096)
    d = np.ascontiguousarray(
        inp["L0_ffn_d"].reshape(M, 128, C, 128).transpose(0, 3, 2, 1)
    ).reshape(M, 128, 7168)
    ws = {}
    for nm, w in (("wq", inp["L0_attn_Wq"]), ("wk", inp["L0_attn_Wk"]),
                  ("wv", inp["L0_attn_Wv"]), ("wo", inp["L0_attn_Wo"])):
        ws[nm] = np.ascontiguousarray(
            w.reshape(M, 128, KK, 128).transpose(0, 3, 2, 1)
        ).reshape(M, 128, 4096)
    gw = np.ascontiguousarray(
        inp["L0_ffn_gate_w"].T.reshape(KK, 128, NE).transpose(1, 0, 2)
    ).reshape(128, 512)
    bs = np.ascontiguousarray(
        np.stack([b.reshape(M, 128).T for b in
                  (inp["L0_attn_bq"], inp["L0_attn_bk"], inp["L0_attn_bv"],
                   inp["L0_attn_bo"])], axis=1)).reshape(128, 128)
    return dict(wg=g, wu=u, wd=d, gw=gw, bs=bs, **ws)


def kernel(**inputs):
    inputs = {k: np.asarray(v, dtype=np.float32) for k, v in inputs.items()}
    w = _prep_weights(inputs)
    aux = dict(ident_d=np.eye(128, dtype=np.float32),
               onesr_d=np.ones((1, 128), np.float32),
               onesc_d=np.ones((128, 1), np.float32))
    x = inputs["L0_attn_x"][0]
    nc = build_nc()
    in_maps = []
    for c in range(NCORE):
        xc = np.ascontiguousarray(
            x[c * NT:(c + 1) * NT, :].T.reshape(M, 128, NT).transpose(1, 0, 2)
        ).reshape(128, M * NT)
        in_maps.append(dict(xin=xc, **aux, **w))
    res = run_bass_kernel_spmd(nc, in_maps, core_ids=list(range(NCORE)))
    out = np.empty((1, NCORE * NT, 4096), np.float32)
    for c in range(NCORE):
        yc = res.results[c]["y"].reshape(128, M, NT)
        out[0, c * NT:(c + 1) * NT, :] = yc.transpose(2, 1, 0).reshape(NT, 4096)
    return out


# version 08
# speedup vs baseline: 1.1842x; 1.0312x over previous
import dataclasses
import numpy as np
import concourse.bass as bass
import concourse.mybir as mybir
from concourse.bass_utils import run_bass_kernel_spmd

F32 = mybir.dt.float32
BF16 = mybir.dt.bfloat16
AF = mybir.ActivationFunctionType
ALU = mybir.AluOpType
AX = mybir.AxisListType

NCORE = 8
NT = 1024
HH = 512
M = 32
C = 56
KK = 32
NE = 16
INV_SQRT_D = float(1.0 / np.sqrt(128.0))
THR = 0.001
LN_EPS = 1e-5
PHLIM = 99
AVSTEP = 6
DBG_X = False


class Em:

    QENG = {"wd0": "pool", "wd1": "pool", "wd2": "pool", "wp": "pool",
            "sd0": "sp", "sd1": "sp", "sd2": "sp", "sd3": "sp",
            "pk0": "sp", "pk1": "sp", "rl": "sp"}

    def __init__(self, nc, mode, sems):
        self.nc = nc
        self.mode = mode
        self.sems = sems
        self.c = {k: 0 for k in
                  ("pe", "dve", "act", "pool", "sp") + tuple(self.QENG)}
        self.last = {}

    def _eng(self, name):
        return {"pe": self.nc.tensor, "dve": self.nc.vector,
                "act": self.nc.scalar, "pool": self.nc.gpsimd,
                "sp": self.nc.sync}[name]

    def op(self, eng, fn):
        if self.mode == eng:
            self.last[eng] = fn(self._eng(eng))

    def mark(self, eng):
        self.c[eng] += 1
        if self.mode == eng:
            self.last[eng].then_inc(self.sems[eng], 1)
        return self.c[eng]

    def dma(self, queue, fn):
        eng = self.QENG[queue]
        self.c[queue] += 16
        if self.mode == eng:
            fn(self._eng(eng)).then_inc(self.sems[queue], 16)
        return self.c[queue]

    def wait(self, eng, sem, val):
        if val <= 0 or eng == sem:
            return
        if self.mode == eng:
            self._eng(eng).wait_ge(self.sems[sem], val)

    def fence(self, eng):
        self.c[eng] += 1
        if self.mode == eng:
            self.last[eng].then_inc(self.sems[eng], 1)
            self._eng(eng).wait_ge(self.sems[eng], self.c[eng])
        return self.c[eng]

    def waitp(self, eng, pair):
        self.wait(eng, pair[0], pair[1])


def repl(ap, dims):
    return dataclasses.replace(ap, ap=[ap.ap[0]] + dims)


def bcast0(ap, reps):
    return dataclasses.replace(ap, ap=list(ap.ap) + [[0, reps]])


def seg32(ap):
    return repl(ap, [[32, 16], [1, 32]])


def build_schedule(em, V):
    S = {}
    regA, regB = V["regA"], V["regB"]
    stage, stagebf, miscbf = V["stage"], V["stagebf"], V["miscbf"]
    sc, gsc = V["sc"], V["gsc"]
    PB, PBT, PBW = V["pb"], V["pbt"], V["pbw"]

    NSLOT = 3
    ring = {"i": 0, "done": [], "cons": []}

    def wslab_push(src_ap):
        i = ring["i"]
        if i >= NSLOT:
            em.wait("pool", "pe", ring["cons"][i - NSLOT])
        slot = i % NSLOT
        w = src_ap.shape[1]
        dst = V["wring"][:, 2048 * slot: 2048 * slot + w]
        v = em.dma("wd%d" % slot,
                   lambda g, d=dst, s=src_ap: g.dma_start(out=d, in_=s))
        ring["done"].append(v)
        ring["cons"].append(None)
        ring["i"] += 1
        return i

    def wslab_wait(i):
        em.wait("pe", "wd%d" % (i % NSLOT), ring["done"][i])

    def wtile(i, t):
        slot = i % NSLOT
        return V["wring"][:, 2048 * slot + 128 * t: 2048 * slot + 128 * t + 128]

    def wslab_done(i):
        ring["cons"][i] = em.mark("pe")

    free = {b: ("dve", 0) for b in range(8)}

    def bank_wait(b, eng="pe"):
        em.waitp(eng, free[b])

    def bank_free(b, sem):
        free[b] = (sem, em.c[sem])

    bfbank = [PBT, PBW]
    bfg = [("dve", 0), ("dve", 0)]
    vp_free = [("pe", 0)] * 4
    wtok_free = [("pe", 0)]
    sc_free = [("dve", 0)]

    stg = {("stage", 0): ("dve", 0), ("stage", 1): ("dve", 0),
           ("stage", 2): ("dve", 0),
           ("sbf", 0): ("dve", 0), ("sbf", 1): ("dve", 0),
           ("mbf", 0): ("dve", 0), ("mbf", 1): ("dve", 0)}

    def slot_wait(key, eng):
        em.waitp(eng, stg[key])

    def slot_set(key, sem):
        stg[key] = (sem, em.c[sem])

    def stg_slot(slot):
        return stage[:, 512 * slot:512 * slot + 512]

    def sbf_slot(slot):
        return stagebf[:, 512 * slot:512 * slot + 512]

    def mbf_slot(slot):
        return miscbf[:, 512 * slot:512 * slot + 512]

    def xln(k, hh):
        return regA[:, k * NT + hh * HH: k * NT + hh * HH + HH]

    def hv(c, hh):
        return regB[:, c * NT + hh * HH: c * NT + hh * HH + HH]

    em.dma("rl", lambda s: s.dma_start(out=V["biases"][:], in_=V["d_b"]))
    pre_if = em.dma("rl", lambda s: s.dma_start(out=V["identF"][:],
                                                in_=V["d_id"][0:16, 0:16]))
    em.dma("wp", lambda g: g.dma_start(out=V["identB"][:], in_=V["d_id"]))
    em.dma("wp", lambda g: g.dma_start(out=V["gatewT"][:], in_=V["d_gw"]))
    em.dma("wp", lambda g: g.dma_start(out=V["onesr"][:], in_=V["d_or"]))
    pre_oc = em.dma("wp", lambda g: g.dma_start(out=V["onesc"][:],
                                                in_=V["d_oc"]))
    for eng in ("pe", "dve", "act"):
        em.wait(eng, "wp", pre_oc)
        em.wait(eng, "rl", pre_if)
    em.op("dve", lambda v: v.memset(V["epsc"][:], LN_EPS))
    eps_ready = em.mark("dve")
    em.wait("act", "dve", eps_ready)

    def stats_unit(m, hh, slot, first):
        pass

    def emit_stats_mms(m, hh, slot, first):
        em.op("pe", lambda t: t.matmul(
            PB[2 + hh][0:1, :], V["onesc"][:], sbf_slot(slot),
            start=first, stop=(m == M - 1), skip_group_check=True))
        em.op("pe", lambda t: t.matmul(
            PB[4 + hh][0:1, :], V["onesc"][:], mbf_slot(slot),
            start=first, stop=(m == M - 1), skip_group_check=True))
        em.mark("pe")
        slot_set(("sbf", slot), "pe")
        slot_set(("mbf", slot), "pe")

    def ln_stats_pass(src):
        for b in (2, 3, 4, 5):
            bank_wait(b, "pe")
        first = True
        units = [(m, hh) for m in range(M) for hh in (0, 1)]
        pend = {}

        def issue(u):
            m, hh = units[u]
            slot = u % 3
            slot_wait(("stage", slot), "sp")
            pend[u] = (em.dma("sd%d" % slot, lambda s: s.dma_start(
                out=stg_slot(slot),
                in_=src[:, m * NT + hh * HH: m * NT + hh * HH + HH])), slot)

        issue(0)
        issue(1)
        for u, (m, hh) in enumerate(units):
            if True:
                if u + 2 < len(units):
                    issue(u + 2)
                dv, slot = pend.pop(u)
                em.wait("dve", "sd%d" % slot, dv)
                slot_wait(("sbf", slot % 2), "dve")
                em.op("dve", lambda v: v.tensor_copy(sbf_slot(slot % 2),
                                                     stg_slot(slot)))
                cm = em.mark("dve")
                slot_set(("stage", slot), "dve")
                em.wait("act", "dve", cm)
                slot_wait(("mbf", slot % 2), "act")
                em.op("act", lambda a: a.activation(
                    mbf_slot(slot % 2), sbf_slot(slot % 2), AF.Square))
                sq = em.mark("act")
                em.wait("pe", "act", sq)
                emit_stats_mms(m, hh, slot % 2, m == 0)
        S["ln_stats_done"] = em.c["pe"]

    def ln_finalize_apply(src, src_ready):
        t_mean = sc[0:1, :]
        t_var = stage[0:1, 0:512]
        t_tmp = stage[0:1, 512:1024]
        for hh in (0, 1):
            em.wait("dve", "pe", S["ln_stats_done"])
            em.waitp("dve", sc_free[0])
            slot_wait(("stage", 0), "dve")
            slot_wait(("stage", 1), "dve")
            em.op("dve", lambda v: v.tensor_scalar(
                t_mean, PB[2 + hh][0:1, :], 1.0 / 4096.0, None, ALU.mult))
            em.fence("dve")
            em.op("dve", lambda v: v.tensor_scalar(
                t_var, PB[4 + hh][0:1, :], 1.0 / 4096.0, None, ALU.mult))
            em.fence("dve")
            em.op("dve", lambda v: v.tensor_mul(t_tmp, t_mean, t_mean))
            em.fence("dve")
            em.op("dve", lambda v: v.tensor_sub(t_var, t_var, t_tmp))
            em.fence("dve")
            em.op("dve", lambda v: v.tensor_copy(V["wrow16"][0:1, 0:512], t_mean))
            f1 = em.mark("dve")
            em.wait("act", "dve", f1)
            em.op("act", lambda a: a.activation(t_tmp, t_var, AF.Sqrt,
                                                bias=V["epsc"][0:1, :]))
            f2 = em.mark("act")
            em.wait("dve", "act", f2)
            em.op("dve", lambda v: v.reciprocal(t_var, t_tmp))
            em.fence("dve")
            em.op("dve", lambda v: v.tensor_copy(V["wrow16"][0:1, 512:1024], t_var))
            f3 = em.mark("dve")
            sc_free[0] = ("dve", em.c["dve"])
            slot_set(("stage", 0), "dve")
            slot_set(("stage", 1), "dve")
            bank_free(2 + hh, "dve")
            bank_free(4 + hh, "dve")
            em.wait("pe", "dve", f3)
            bank_wait(0, "pe")
            bank_wait(1, "pe")
            em.op("pe", lambda t: t.matmul(PB[0][:], V["onesr"][:],
                                           V["wrow16"][0:1, 0:512], start=True, stop=True))
            em.op("pe", lambda t: t.matmul(PB[1][:], V["onesr"][:],
                                           V["wrow16"][0:1, 512:1024], start=True, stop=True))
            bc = em.mark("pe")
            em.wait("dve", "pe", bc)
            slot_wait(("sbf", 0), "dve")
            slot_wait(("sbf", 1), "dve")
            em.op("dve", lambda v: v.tensor_copy(stagebf[:, 0:512], PB[0][:]))
            em.op("dve", lambda v: v.tensor_copy(stagebf[:, 512:1024],
                                                 PB[1][:]))
            mrev = em.mark("dve")
            em.wait("pool", "dve", mrev)
            apend = {}

            def aissue(m, hh=hh):
                slot = m % 3
                slot_wait(("stage", slot), "sp")
                apend[m] = (em.dma("sd%d" % slot, lambda s: s.dma_start(
                    out=stg_slot(slot),
                    in_=src[:, m * NT + hh * HH: m * NT + hh * HH + HH])),
                    slot)

            if src_ready:
                for q in ("sd0", "sd1", "sd2", "sd3"):
                    em.wait("sp", q, em.c[q])
                src_ready = False
            aissue(0)
            aissue(1)
            for m in range(M):
                if m + 2 < M:
                    aissue(m + 2)
                dv, slot = apend.pop(m)
                eng = "pool" if m % 3 == 2 else "dve"
                em.wait(eng, "sd%d" % slot, dv)
                if eng == "dve":
                    em.wait("dve", "pe", bc)
                    em.op("dve", lambda v: v.tensor_sub(
                        stg_slot(slot), stg_slot(slot), PB[0][:]))
                    em.op("dve", lambda v: v.tensor_mul(
                        xln(m, hh), stg_slot(slot), PB[1][:]))
                else:
                    em.op("pool", lambda g: g.tensor_sub(
                        stg_slot(slot), stg_slot(slot), stagebf[:, 0:512]))
                    em.op("pool", lambda g: g.tensor_mul(
                        xln(m, hh), stg_slot(slot), stagebf[:, 512:1024]))
                em.mark(eng)
                slot_set(("stage", slot), eng)
            bank_free(0, "dve")
            bank_free(1, "dve")
            slot_set(("sbf", 0), "pool")
            slot_set(("sbf", 1), "pool")
        S["xln_ready"] = [("dve", em.c["dve"]), ("pool", em.c["pool"])]

    def gating():
        for pr in S["xln_ready"]:
            em.waitp("pe", pr)
        pass
        bank_wait(2, "pe")
        bank_wait(3, "pe")
        for hh in (0, 1):
            for k in range(KK):
                em.op("pe", lambda t: t.matmul(
                    PB[2 + hh][0:16, :], V["gatewT"][:, 16 * k:16 * k + 16],
                    xln(k, hh), start=(k == 0), stop=(k == KK - 1)))
        lg = em.mark("pe")
        em.wait("dve", "pe", lg)
        slot_wait(("stage", 0), "dve")
        slot_wait(("stage", 1), "dve")
        em.op("dve", lambda v: v.tensor_copy(stage[0:16, 0:512],
                                             PB[2][0:16, :]))
        em.op("dve", lambda v: v.tensor_copy(stage[0:16, 512:1024],
                                             PB[3][0:16, :]))
        lge = em.mark("dve")
        bank_free(2, "dve")
        bank_free(3, "dve")
        em.wait("pe", "dve", lge)
        bank_wait(4, "pe")
        for b in range(8):
            hh, bb = divmod(b, 4)
            em.op("pe", lambda t: t.transpose(
                PB[4][:, 16 * b:16 * b + 16],
                stage[0:16, hh * 512 + 128 * bb: hh * 512 + 128 * bb + 128],
                V["identF"][:]))
        ltr = em.mark("pe")
        em.wait("dve", "pe", ltr)
        em.op("dve", lambda v: v.tensor_copy(V["ltok"][:], PB[4][:, 0:128]))
        em.mark("dve")
        bank_free(4, "dve")
        slot_set(("stage", 0), "dve")
        slot_set(("stage", 1), "dve")
        em.waitp("dve", wtok_free[0])
        for b in range(8):
            lt = V["ltok"][:, 16 * b:16 * b + 16]
            c_m1, c_m2 = gsc[:, 0:1], gsc[:, 1:2]
            c_e2, c_den, c_r = gsc[:, 2:3], gsc[:, 3:4], gsc[:, 4:5]
            r_t, r_u, r_x = gsc[:, 16:32], gsc[:, 32:48], gsc[:, 48:64]
            em.op("dve", lambda v: v.reduce_max(c_m1, lt, axis=AX.X))
            em.fence("dve")
            em.op("dve", lambda v: v.tensor_scalar(r_t, lt, c_m1, None,
                                                   ALU.subtract))
            em.fence("dve")
            em.op("dve", lambda v: v.tensor_scalar(r_u, r_t, 0.0, None,
                                                   ALU.is_ge))
            em.fence("dve")
            em.op("dve", lambda v: v.scalar_tensor_tensor(
                r_x, r_u, -1e30, r_t, ALU.mult, ALU.add))
            em.fence("dve")
            em.op("dve", lambda v: v.reduce_max(c_m2, r_x, axis=AX.X))
            g1 = em.mark("dve")
            em.wait("act", "dve", g1)
            em.op("act", lambda a: a.activation(c_e2, c_m2, AF.Exp))
            em.fence("act")
            em.op("act", lambda a: a.activation(r_x, r_t, AF.Exp))
            g2 = em.mark("act")
            em.wait("dve", "act", g2)
            em.op("dve", lambda v: v.tensor_scalar(c_den, c_e2, 1.0, None,
                                                   ALU.add))
            em.fence("dve")
            em.op("dve", lambda v: v.reciprocal(c_r, c_den))
            em.fence("dve")
            em.op("dve", lambda v: v.tensor_scalar(r_u, r_t, c_m2, None,
                                                   ALU.is_ge))
            em.fence("dve")
            em.op("dve", lambda v: v.tensor_mul(r_t, r_u, r_x))
            em.fence("dve")
            em.op("dve", lambda v: v.tensor_scalar(
                V["wtok"][:, 16 * b:16 * b + 16], r_t, c_r, 0.5,
                ALU.mult, ALU.mult))
            em.mark("dve")
        S["wtok_ready"] = em.c["dve"]

    fin_at = {}
    for e in range(NE):
        fin_at.setdefault((448 * e + 447) // 128, []).append(e)

    def expert_spans(e):
        lo, hi = 448 * e, 448 * (e + 1)
        spans = []
        c0 = lo // 128
        if lo % 128:
            spans.append((slice(64, 128), c0, 1))
            c0 += 1
        nfull = hi // 128 - c0
        if nfull:
            spans.append((slice(0, 128), c0, nfull))
        if hi % 128:
            spans.append((slice(0, 64), hi // 128, 1))
        return spans

    def hspan_ap(psl, c0, hh, nch):
        base = regB[psl, c0 * NT + hh * HH: c0 * NT + hh * HH + HH]
        return repl(base, [[NT, nch], [1, HH]])

    def wspan_ap(psl, nch, hh2):
        return repl(PB[2 + hh2][psl, :], [[0, nch], [1, HH]])

    def ffn_gu(wg_src, wu_src):
        hmul = {}
        wpend = {}
        for c in range(C):
            sg0 = wslab_push(wg_src[c, :, 0:2048])
            sg1 = wslab_push(wg_src[c, :, 2048:4096])
            su0 = wslab_push(wu_src[c, :, 0:2048])
            gb0, gb1 = PB[0], PB[1]
            for e in fin_at.get(c, []):
                em.wait("pe", "dve", S["wtok_ready"])
                em.waitp("pe", bfg[0])
                for b in range(8):
                    em.op("pe", lambda t: t.transpose(
                        PBT[0:1, 128 * b:128 * b + 128],
                        V["wtok"][:, 16 * b + e:16 * b + e + 1],
                        V["identB"][:]))
                wtr = em.mark("pe")
                em.wait("dve", "pe", wtr)
                em.op("dve", lambda v: v.tensor_copy(V["wrow16"][0:1, 0:512],
                                                     PBT[0:1, 0:512]))
                em.op("dve", lambda v: v.tensor_copy(
                    V["wrow16"][0:1, 512:1024], PBT[0:1, 512:1024]))
                wre = em.mark("dve")
                bfg[0] = ("dve", em.c["dve"])
                em.wait("pe", "dve", wre)
                wbs = []
                for hh2 in (0, 1):
                    bank_wait(2 + hh2, "pe")
                    em.op("pe", lambda t: t.matmul(
                        PB[2 + hh2][:], V["onesr"][:],
                        V["wrow16"][0:1, 512 * hh2:512 * hh2 + 512],
                        start=True, stop=True))
                    wbs.append(em.mark("pe"))
                wpend[e] = wbs
            em.wait("pe", "dve", S["xln_ready"]) if isinstance(
                S["xln_ready"], int) else [em.waitp("pe", pr)
                                           for pr in S["xln_ready"]]
            bank_wait(0, "pe")
            bank_wait(1, "pe")
            for k in range(KK):
                slab = sg0 if k < 16 else sg1
                if k in (0, 16):
                    wslab_wait(slab)
                em.op("pe", lambda t: t.matmul(
                    gb0[:], wtile(slab, k % 16), xln(k, 0),
                    start=(k == 0), stop=(k == KK - 1)))
                em.op("pe", lambda t: t.matmul(
                    gb1[:], wtile(slab, k % 16), xln(k, 1),
                    start=(k == 0), stop=(k == KK - 1)))
                if k == 15:
                    wslab_done(sg0)
            wslab_done(sg1)
            su1 = wslab_push(wu_src[c, :, 2048:4096])
            gdone = em.c["pe"]
            bank_wait(4, "pe")
            bank_wait(5, "pe")
            for k in range(KK):
                slab = su0 if k < 16 else su1
                if k in (0, 16):
                    wslab_wait(slab)
                em.op("pe", lambda t: t.matmul(
                    PB[4][:], wtile(slab, k % 16), xln(k, 0),
                    start=(k == 0), stop=(k == KK - 1)))
                em.op("pe", lambda t: t.matmul(
                    PB[5][:], wtile(slab, k % 16), xln(k, 1),
                    start=(k == 0), stop=(k == KK - 1)))
                if k == 15:
                    wslab_done(su0)
            wslab_done(su1)
            udone = em.c["pe"]
            for hh in (0, 1):
                gb = gb0 if hh == 0 else gb1
                ub = PB[4 + hh]
                em.wait("act", "pe", gdone)
                slot_wait(("mbf", hh), "act")
                em.op("act", lambda a: a.activation(mbf_slot(hh), gb[:],
                                                    AF.Silu))
                sm = em.mark("act")
                bank_free(hh, "act")
                em.wait("dve", "act", sm)
                em.wait("dve", "pe", udone)
                em.op("dve", lambda v: v.tensor_mul(hv(c, hh), mbf_slot(hh),
                                                    ub[:]))
                hmul[(c, hh)] = em.mark("dve")
                slot_set(("mbf", hh), "dve")
                bank_free(4 + hh, "dve")
            for e in fin_at.get(c, []):
                wbs = wpend.pop(e)
                for hh2 in (0, 1):
                    em.wait("dve", "pe", wbs[hh2])
                    for (psl, c0, nch) in expert_spans(e):
                        em.op("dve", lambda v: v.tensor_mul(
                            hspan_ap(psl, c0, hh2, nch),
                            hspan_ap(psl, c0, hh2, nch),
                            wspan_ap(psl, nch, hh2)))
                    em.mark("dve")
                    bank_free(2 + hh2, "dve")
        wtok_free[0] = ("pe", em.c["pe"])
        S["h_ready"] = em.c["dve"]

    def ffn_down(wd_src, x_src, dst_dram, stats):
        if stats:
            for b in (2, 3, 4, 5):
                bank_wait(b, "pe")
        wv = 0
        dpend = {}

        def dissue(u):
            m2, hh2 = divmod(u, 2)
            slot = u % 3
            slot_wait(("stage", slot), "sp")
            dpend[u] = (em.dma("sd%d" % slot, lambda s: s.dma_start(
                out=stg_slot(slot),
                in_=x_src[:, m2 * NT + hh2 * HH: m2 * NT + hh2 * HH + HH])),
                slot)

        for m in range(M):
            if stats:
                i0, i1 = 0, 1
            else:
                i0, i1 = 2 * (m % 2), 2 * (m % 2) + 1
            b0, b1 = PB[i0], PB[i1]
            sl = [wslab_push(wd_src[m, :, 2048 * q: 2048 * q + 2048])
                  for q in range(3)]
            em.wait("pe", "dve", S["h_ready"])
            bank_wait(i0, "pe")
            bank_wait(i1, "pe")
            for c in range(C):
                q, t = divmod(c, 16)
                if c == 16:
                    sl.append(wslab_push(wd_src[m, :, 6144:7168]))
                if t == 0:
                    wslab_wait(sl[q])
                em.op("pe", lambda te: te.matmul(
                    b0[:], wtile(sl[q], t), hv(c, 0),
                    start=(c == 0), stop=(c == C - 1)))
                em.op("pe", lambda te: te.matmul(
                    b1[:], wtile(sl[q], t), hv(c, 1),
                    start=(c == 0), stop=(c == C - 1)))
                if t == 15 or c == C - 1:
                    wslab_done(sl[q])
            ddone = em.c["pe"]
            for hh in (0, 1):
                bank = b0 if hh == 0 else b1
                u = 2 * m + hh
                if u == 0:
                    dissue(0)
                    dissue(1)
                if u + 2 < 2 * M:
                    dissue(u + 2)
                dv, slot = dpend.pop(u)
                em.wait("dve", "sd%d" % slot, dv)
                em.wait("dve", "pe", ddone)
                em.op("dve", lambda v: v.tensor_add(stg_slot(slot),
                                                    stg_slot(slot), bank[:]))
                if stats:
                    slot_wait(("sbf", slot % 2), "dve")
                    em.op("dve", lambda v: v.tensor_copy(sbf_slot(slot % 2),
                                                         stg_slot(slot)))
                ev = em.mark("dve")
                bank_free(i0 + hh, "dve")
                em.wait("sp", "dve", ev)
                wv = em.dma("sd%d" % slot, lambda s: s.dma_start(
                    out=dst_dram[:, m * NT + hh * HH: m * NT + hh * HH + HH],
                    in_=stg_slot(slot)))
                slot_set(("stage", slot), "sd%d" % slot)
                if stats:
                    em.wait("act", "dve", ev)
                    slot_wait(("mbf", slot % 2), "act")
                    em.op("act", lambda a: a.activation(
                        mbf_slot(slot % 2), sbf_slot(slot % 2), AF.Square))
                    sq = em.mark("act")
                    em.wait("pe", "act", sq)
                    emit_stats_mms(m, hh, slot % 2, m == 0)
        if stats:
            S["ln_stats_done"] = em.c["pe"]
        return wv

    def proj(w_src, kind, rhs_fn, bias_idx, x_src=None, dst_dram=None,
             park=None, stats=False):
        evmarks, parkw = [], []
        wrlast = 0
        opend = {}

        def oissue(u):
            m2, hh2 = divmod(u, 2)
            slot = u % 3
            slot_wait(("stage", slot), "sp")
            opend[u] = (em.dma("sd%d" % slot, lambda s: s.dma_start(
                out=stg_slot(slot),
                in_=x_src[:, m2 * NT + hh2 * HH: m2 * NT + hh2 * HH + HH])),
                slot)

        if stats:
            for b in (2, 3, 4, 5):
                bank_wait(b, "pe")
        for m in range(M):
            s0 = wslab_push(w_src[m, :, 0:2048])
            s1 = wslab_push(w_src[m, :, 2048:4096])
            if stats:
                i0, i1 = 0, 1
            else:
                i0, i1 = 2 * (m % 2), 2 * (m % 2) + 1
            b0, b1 = PB[i0], PB[i1]
            for pr in (S["rhs_ready"] if isinstance(S["rhs_ready"], list)
                       else [S["rhs_ready"]]):
                em.waitp("pe", pr)
            bank_wait(i0, "pe")
            bank_wait(i1, "pe")
            for k in range(KK):
                slab = s0 if k < 16 else s1
                if k in (0, 16):
                    wslab_wait(slab)
                em.op("pe", lambda t: t.matmul(
                    b0[:], wtile(slab, k % 16), rhs_fn(k, 0),
                    start=(k == 0), stop=(k == KK - 1)))
                em.op("pe", lambda t: t.matmul(
                    b1[:], wtile(slab, k % 16), rhs_fn(k, 1),
                    start=(k == 0), stop=(k == KK - 1)))
                if k == 15:
                    wslab_done(s0)
            wslab_done(s1)
            pdone = em.c["pe"]
            bias = V["biases"][:, 32 * bias_idx + m: 32 * bias_idx + m + 1]
            if kind in ("q", "k"):
                base = 0 if kind == "q" else 16384
                em.wait("act", "pe", pdone)
                em.op("act", lambda a: a.activation(
                    regB[:, base + m * HH: base + m * HH + HH], b0[:],
                    AF.Identity, bias=bias))
                e0 = em.mark("act")
                bank_free(i0, "act")
                slot = m % 2
                slot_wait(("mbf", slot), "act")
                em.op("act", lambda a: a.activation(mbf_slot(slot), b1[:],
                                                    AF.Identity, bias=bias))
                e1 = em.mark("act")
                bank_free(i1, "act")
                em.wait("sp", "act", e1)
                pw = em.dma("pk%d" % slot, lambda s: s.dma_start(
                    out=park[:, m * HH: m * HH + HH], in_=mbf_slot(slot)))
                parkw.append(pw)
                slot_set(("mbf", slot), "pk%d" % slot)
                evmarks.append(e1)
            elif kind == "v":
                em.wait("act", "pe", pdone)
                if m == 0:
                    em.wait("act", "pe", S["scores_h1_done"])
                for hh in (0, 1):
                    bank = b0 if hh == 0 else b1
                    base = regB[:, 16384 + hh * 16384 + m:
                                16384 + hh * 16384 + m + 1]
                    dst = repl(base, [[32, HH]])
                    em.op("act", lambda a: a.activation(dst, bank[:], AF.Identity,
                                                        bias=bias))
                    evmarks.append(em.mark("act"))
                    bank_free(i0 + hh, "act")
            elif kind == "o":
                for hh in (0, 1):
                    bank = b0 if hh == 0 else b1
                    slot = (2 * m + hh) % 2
                    em.wait("act", "pe", pdone)
                    em.waitp("act", sc_free[0])
                    em.op("act", lambda a: a.activation(sc[:, :], bank[:],
                                                        AF.Identity, bias=bias))
                    ab = em.mark("act")
                    bank_free(i0 + hh, "act")
                    u = 2 * m + hh
                    if u == 0:
                        oissue(0)
                        oissue(1)
                    if u + 2 < 2 * M:
                        oissue(u + 2)
                    dv, slot = opend.pop(u)
                    em.wait("dve", "sd%d" % slot, dv)
                    em.wait("dve", "act", ab)
                    em.op("dve", lambda v: v.tensor_add(stg_slot(slot),
                                                        stg_slot(slot),
                                                        sc[:, :]))
                    if stats:
                        slot_wait(("sbf", slot % 2), "dve")
                        em.op("dve", lambda v: v.tensor_copy(
                            sbf_slot(slot % 2), stg_slot(slot)))
                    ev = em.mark("dve")
                    sc_free[0] = ("dve", em.c["dve"])
                    em.wait("sp", "dve", ev)
                    wrlast = em.dma("sd%d" % slot, lambda s: s.dma_start(
                        out=dst_dram[:, m * NT + hh * HH:
                                     m * NT + hh * HH + HH],
                        in_=stg_slot(slot)))
                    slot_set(("stage", slot), "sd%d" % slot)
                    if stats:
                        em.wait("act", "dve", ev)
                        slot_wait(("mbf", slot % 2), "act")
                        em.op("act", lambda a: a.activation(
                            mbf_slot(slot % 2), sbf_slot(slot % 2), AF.Square))
                        sq = em.mark("act")
                        em.wait("pe", "act", sq)
                        emit_stats_mms(m, hh, slot % 2, m == 0)
        if stats:
            S["ln_stats_done"] = em.c["pe"]
        return evmarks, parkw, wrlast

    def scores_half(half, qk_ready):
        for kb in range(8):
            t0 = 64 * kb
            bank = PB[kb % 2]
            bank_wait(kb % 2, "pe")
            em.waitp("pe", qk_ready)
            for s in range(64):
                a = s % 4
                b = s // 4
                tl = t0 + s
                qs = regB[:, tl:16384:HH]
                ks = regB[:, 16384 + tl:32768:HH]
                em.op("pe", lambda t: t.matmul(
                    bank[32 * a:32 * a + 32, 32 * b:32 * b + 32], qs, ks,
                    start=True, stop=True,
                    tile_position=(0, 32 * a), skip_group_check=True))
            mm = em.mark("pe")
            em.wait("dve", "pe", mm)
            em.waitp("dve", sc_free[0])
            em.op("dve", lambda v: v.tensor_scalar(
                sc[:, :], bank[:], THR / INV_SQRT_D, 100.0, ALU.is_gt,
                ALU.mult))
            em.fence("dve")
            em.op("dve", lambda v: v.scalar_tensor_tensor(
                sc[:, :], bank[:], INV_SQRT_D, sc[:, :], ALU.mult, ALU.add))
            em.fence("dve")
            em.op("dve", lambda v: v.reduce_max(gsc[:, 0:16],
                                                seg32(sc[:, :]), axis=AX.X))
            em.fence("dve")
            em.op("dve", lambda v: v.tensor_sub(
                seg32(sc[:, :]), seg32(sc[:, :]), bcast0(gsc[:, 0:16], 32)))
            d1 = em.mark("dve")
            bank_free(kb % 2, "dve")
            em.wait("act", "dve", d1)
            slot_wait(("sbf", 0), "act")
            em.op("act", lambda a: a.activation(stagebf[:, 0:512], sc[:, :],
                                                AF.Exp))
            ex = em.mark("act")
            sc_free[0] = ("act", em.c["act"])
            em.wait("dve", "act", ex)
            slot_wait(("sbf", 1), "dve")
            em.op("dve", lambda v: v.reduce_sum(
                gsc[:, 16:32], seg32(stagebf[:, 0:512]), axis=AX.X))
            em.fence("dve")
            em.op("dve", lambda v: v.reciprocal(gsc[:, 32:48], gsc[:, 16:32]))
            em.fence("dve")
            em.op("dve", lambda v: v.tensor_mul(
                seg32(stagebf[:, 512:1024]), seg32(stagebf[:, 0:512]),
                bcast0(gsc[:, 32:48], 32)))
            em.fence("dve")
            atcol = 49152 + (8 * half + kb) * 512
            for j in range(16):
                em.op("dve", lambda v: v.transpose(
                    regB[:, atcol + 32 * j: atcol + 32 * j + 32],
                    stagebf[:, 512 + 32 * j: 512 + 32 * j + 32]))
            em.mark("dve")
            slot_set(("sbf", 0), "dve")
            slot_set(("sbf", 1), "dve")
        S["scores_done_%d" % half] = em.c["pe"]
        S["attnT_done_%d" % half] = em.c["dve"]

    def av_phase():
        em.wait("pe", "act", S["v_evac_done"])
        em.wait("pe", "dve", S["attnT_done_1"])
        for grp in range(64):
            t0 = 16 * grp
            avb = 4 + grp % 2
            avmm = 0
            vpes = []
            for i in range(4):
                bk = i % 2
                em.waitp("pe", bfg[bk])
                em.op("pe", lambda t: t.transpose(
                    bfbank[bk][:, 0:128],
                    regB[:, 16384 + (t0 + 4 * i) * 32:
                         16384 + (t0 + 4 * i) * 32 + 128],
                    V["identB"][:]))
                vtr = em.mark("pe")
                em.wait("act", "pe", vtr)
                em.waitp("act", vp_free[i])
                em.op("act", lambda a: a.activation(
                    V["vpack"][:, 128 * i:128 * i + 128],
                    bfbank[bk][:, 0:128], AF.Identity))
                vpes.append(em.mark("act"))
                bfg[bk] = ("act", em.c["act"])
            bank_wait(avb, "pe")
            for i in range(4):
                em.wait("pe", "act", vpes[i])
                for j in range(4):
                    tg = t0 + 4 * i + j
                    kbg = tg // 64
                    s64 = tg % 64
                    acol = 49152 + kbg * 512 + 32 * (s64 // 4)
                    em.op("pe", lambda t: t.matmul(
                        PB[avb][32 * j:32 * j + 32, 128 * i:128 * i + 128],
                        regB[32 * j:32 * j + 32, acol:acol + 32],
                        V["vpack"][32 * j:32 * j + 32,
                                   128 * i:128 * i + 128],
                        start=True, stop=True,
                        tile_position=(32 * j, 32 * j),
                        skip_group_check=True))
                avmm = em.mark("pe")
                vp_free[i] = ("pe", em.c["pe"])
            em.wait("dve", "pe", avmm)
            slot_wait(("mbf", 0), "dve")
            em.op("dve", lambda v: v.tensor_copy(miscbf[:, 0:512],
                                                 PB[avb][:]))
            ae = em.mark("dve")
            bank_free(avb, "dve")
            for i in range(4):
                bk = i % 2
                em.waitp("pe", bfg[bk])
                em.wait("pe", "dve", ae)
                em.op("pe", lambda t: t.transpose(
                    bfbank[bk][:, 128:256],
                    miscbf[:, 128 * i:128 * i + 128], V["identB"][:]))
                otr = em.mark("pe")
                em.wait("dve", "pe", otr)
                srcap = repl(bfbank[bk][:, 128:129], [[32, 4], [1, 32]])
                dst = repl(regA[:, t0 + 4 * i: t0 + 4 * i + 1],
                           [[1, 4], [NT, 32]])
                em.op("dve", lambda v: v.tensor_copy(dst, srcap))
                em.mark("dve")
                bfg[bk] = ("dve", em.c["dve"])
            slot_set(("mbf", 0), "dve")
        S["ao_ready"] = em.c["dve"]

    def fin():
        if DBG_X:
            em.wait("sp", "pe", em.c["pe"])
            em.wait("sp", "dve", em.c["dve"])
            em.wait("sp", "act", em.c["act"])
            em.dma("rl", lambda s: s.dma_start(out=V["d_dbgA"], in_=regA[:, :]))
            em.dma("rl", lambda s: s.dma_start(out=V["d_dbgB"], in_=regB[:, :]))
            em.dma("rl", lambda s: s.dma_start(out=V["d_dbgW"],
                                               in_=V["wtok"][:, :]))
            em.dma("rl", lambda s: s.dma_start(out=V["d_dbgL"],
                                               in_=V["ltok"][:, :]))
        for q in Em.QENG:
            em.wait("sp", q, em.c[q])

    ln_stats_pass(V["d_x"])
    if PHLIM <= 0:
        return fin()
    ln_finalize_apply(V["d_x"], False)
    if PHLIM <= 1:
        return fin()
    gating()
    if PHLIM <= 2:
        return fin()
    ffn_gu(V["d_g"], V["d_u"])
    if PHLIM <= 3:
        return fin()
    x1w = ffn_down(V["d_d"], V["d_x"], V["d_x1"], stats=True)
    if PHLIM <= 4:
        return fin()
    ln_finalize_apply(V["d_x1"], True)
    S["rhs_ready"] = S["xln_ready"]
    if PHLIM <= 5:
        return fin()
    qe, qpw, _ = proj(V["d_wq"], "q", xln, 0, park=V["d_qp"])
    ke, kpw, _ = proj(V["d_wk"], "k", xln, 1, park=V["d_kp"])
    if PHLIM <= 6:
        return fin()
    scores_half(0, ("act", ke[-1]))
    if PHLIM <= 7:
        return fin()
    em.wait("sp", "pk0", em.c["pk0"])
    em.wait("sp", "pk1", em.c["pk1"])
    em.wait("sp", "pe", S["scores_done_0"])
    em.dma("rl", lambda s: s.dma_start(out=regB[:, 0:16384],
                                       in_=V["d_qp"][:, :]))
    rl2 = em.dma("rl", lambda s: s.dma_start(out=regB[:, 16384:32768],
                                             in_=V["d_kp"][:, :]))
    scores_half(1, ("rl", rl2))
    S["scores_h1_done"] = S["scores_done_1"]
    if PHLIM <= 8:
        return fin()
    ve, _, _ = proj(V["d_wv"], "v", xln, 2)
    S["v_evac_done"] = ve[-1]
    if PHLIM <= 9:
        return fin()
    av_phase()
    S["rhs_ready"] = [("dve", S["ao_ready"])]
    if PHLIM <= 10:
        return fin()

    def ao_rhs(k, hh):
        return regA[:, k * NT + hh * HH: k * NT + hh * HH + HH]

    _, _, x2w = proj(V["d_wo"], "o", ao_rhs, 3, x_src=V["d_x1"],
                     dst_dram=V["d_x2"], stats=True)
    if PHLIM <= 11:
        return fin()
    ln_finalize_apply(V["d_x2"], True)
    if PHLIM <= 12:
        return fin()
    gating()
    ffn_gu(V["d_g"], V["d_u"])
    if PHLIM <= 13:
        return fin()
    ffn_down(V["d_d"], V["d_x2"], V["d_y"], stats=False)
    fin()


def build_nc():
    nc = bass.Bass("TRN2", target_bir_lowering=False,
                   detect_race_conditions=False)
    V = {}
    V["d_x"] = nc.dram_tensor("xin", [128, M * NT], F32,
                              kind="ExternalInput").ap()
    V["d_y"] = nc.dram_tensor("y", [128, M * NT], F32,
                              kind="ExternalOutput").ap()
    V["d_g"] = nc.dram_tensor("wg", [C, 128, 4096], F32,
                              kind="ExternalInput").ap()
    V["d_u"] = nc.dram_tensor("wu", [C, 128, 4096], F32,
                              kind="ExternalInput").ap()
    V["d_d"] = nc.dram_tensor("wd", [M, 128, 7168], F32,
                              kind="ExternalInput").ap()
    for nm in ("wq", "wk", "wv", "wo"):
        V["d_" + nm] = nc.dram_tensor(nm, [M, 128, 4096], F32,
                                      kind="ExternalInput").ap()
    V["d_gw"] = nc.dram_tensor("gw", [128, 512], F32,
                               kind="ExternalInput").ap()
    V["d_b"] = nc.dram_tensor("bs", [128, 128], F32,
                              kind="ExternalInput").ap()
    V["d_id"] = nc.dram_tensor("ident_d", [128, 128], F32,
                               kind="ExternalInput").ap()
    V["d_or"] = nc.dram_tensor("onesr_d", [1, 128], F32,
                               kind="ExternalInput").ap()
    V["d_oc"] = nc.dram_tensor("onesc_d", [128, 1], F32,
                               kind="ExternalInput").ap()
    if DBG_X:
        V["d_dbgA"] = nc.dram_tensor("dbgA", [128, 32768], BF16,
                                     kind="ExternalOutput").ap()
        V["d_dbgB"] = nc.dram_tensor("dbgB", [128, 57344], BF16,
                                     kind="ExternalOutput").ap()
        V["d_dbgW"] = nc.dram_tensor("dbgW", [128, 128], BF16,
                                     kind="ExternalOutput").ap()
        V["d_dbgL"] = nc.dram_tensor("dbgL", [128, 128], F32,
                                     kind="ExternalOutput").ap()
    xkind = "ExternalOutput" if DBG_X else "Internal"
    V["d_x1"] = nc.dram_tensor("x1d", [128, M * NT], F32, kind=xkind).ap()
    V["d_x2"] = nc.dram_tensor("x2d", [128, M * NT], F32, kind=xkind).ap()
    V["d_qp"] = nc.dram_tensor("qpark", [128, 16384], BF16,
                               kind="Internal").ap()
    V["d_kp"] = nc.dram_tensor("kpark", [128, 16384], BF16,
                               kind="Internal").ap()

    guards = []

    def sb(name, shape, dtype):
        g = nc.sbuf_tensor(name, shape, dtype)
        t = g.__enter__()
        guards.append(g)
        return t

    def ps(name, shape, dtype):
        g = nc.psum_tensor(name, shape, dtype)
        t = g.__enter__()
        guards.append(g)
        return t

    V["regA"] = sb("regA", [128, 32768], BF16)
    V["regB"] = sb("regB", [128, 57344], BF16)
    V["wring"] = sb("wring", [128, 6144], BF16)
    V["stage"] = sb("stage", [128, 1536], F32)
    V["sc"] = sb("sc", [128, 512], F32)
    V["stagebf"] = sb("stagebf", [128, 1024], BF16)
    V["miscbf"] = sb("miscbf", [128, 1024], BF16)
    V["vpack"] = sb("vpack", [128, 512], BF16)
    V["wrow16"] = sb("wrow16", [1, 1024], BF16)
    V["gsc"] = sb("gsc", [128, 64], F32)
    V["ltok"] = sb("ltok", [128, 128], F32)
    V["wtok"] = sb("wtok", [128, 128], BF16)
    V["gatewT"] = sb("gatewT", [128, 512], BF16)
    V["biases"] = sb("biases", [128, 128], F32)
    V["identB"] = sb("identB", [128, 128], BF16)
    V["identF"] = sb("identF", [16, 16], F32)
    V["onesr"] = sb("onesr", [1, 128], BF16)
    V["epsc"] = sb("epsc", [128, 1], F32)
    V["onesc"] = sb("onesc", [128, 1], BF16)

    V["pb"] = [ps("pb%d" % i, [128, 512], F32) for i in range(6)]
    V["pbt"] = ps("pbt", [128, 1024], BF16)
    V["pbw"] = ps("pbw", [128, 1024], BF16)

    sems = {}
    for s in ("pe", "dve", "act", "pool", "sp", "wd0", "wd1", "wd2", "wp",
              "sd0", "sd1", "sd2", "sd3", "pk0", "pk1", "rl"):
        g = nc.semaphore(s + "_sem")
        sems[s] = g.__enter__()
        guards.append(g)

    blk = nc.Block()
    block = blk.__enter__()
    for mode, deco in (("pool", block.gpsimd), ("pe", block.tensor),
                       ("dve", block.vector), ("act", block.scalar),
                       ("sp", block.sync)):
        def body(eng, mode=mode):
            build_schedule(Em(nc, mode, sems), V)
        deco(body)
    blk.__exit__(None, None, None)
    for g in reversed(guards):
        g.__exit__(None, None, None)
    return nc


def _prep_weights(inp):
    g = np.ascontiguousarray(
        inp["L0_ffn_g"].reshape(C, 128, KK, 128).transpose(0, 3, 2, 1)
    ).reshape(C, 128, 4096)
    u = np.ascontiguousarray(
        inp["L0_ffn_u"].reshape(C, 128, KK, 128).transpose(0, 3, 2, 1)
    ).reshape(C, 128, 4096)
    d = np.ascontiguousarray(
        inp["L0_ffn_d"].reshape(M, 128, C, 128).transpose(0, 3, 2, 1)
    ).reshape(M, 128, 7168)
    ws = {}
    for nm, w in (("wq", inp["L0_attn_Wq"]), ("wk", inp["L0_attn_Wk"]),
                  ("wv", inp["L0_attn_Wv"]), ("wo", inp["L0_attn_Wo"])):
        ws[nm] = np.ascontiguousarray(
            w.reshape(M, 128, KK, 128).transpose(0, 3, 2, 1)
        ).reshape(M, 128, 4096)
    gw = np.ascontiguousarray(
        inp["L0_ffn_gate_w"].T.reshape(KK, 128, NE).transpose(1, 0, 2)
    ).reshape(128, 512)
    bs = np.ascontiguousarray(
        np.stack([b.reshape(M, 128).T for b in
                  (inp["L0_attn_bq"], inp["L0_attn_bk"], inp["L0_attn_bv"],
                   inp["L0_attn_bo"])], axis=1)).reshape(128, 128)
    return dict(wg=g, wu=u, wd=d, gw=gw, bs=bs, **ws)


def kernel(**inputs):
    inputs = {k: np.asarray(v, dtype=np.float32) for k, v in inputs.items()}
    w = _prep_weights(inputs)
    aux = dict(ident_d=np.eye(128, dtype=np.float32),
               onesr_d=np.ones((1, 128), np.float32),
               onesc_d=np.ones((128, 1), np.float32))
    x = inputs["L0_attn_x"][0]
    nc = build_nc()
    in_maps = []
    for c in range(NCORE):
        xc = np.ascontiguousarray(
            x[c * NT:(c + 1) * NT, :].T.reshape(M, 128, NT).transpose(1, 0, 2)
        ).reshape(128, M * NT)
        in_maps.append(dict(xin=xc, **aux, **w))
    res = run_bass_kernel_spmd(nc, in_maps, core_ids=list(range(NCORE)))
    out = np.empty((1, NCORE * NT, 4096), np.float32)
    for c in range(NCORE):
        yc = res.results[c]["y"].reshape(128, M, NT)
        out[0, c * NT:(c + 1) * NT, :] = yc.transpose(2, 1, 0).reshape(NT, 4096)
    return out
